# version 16
# baseline (speedup 1.0000x reference)
import numpy as np
import ml_dtypes
from contextlib import ExitStack

import concourse.bass as bass
import concourse.mybir as mybir
from concourse.bass_utils import run_bass_kernel_spmd

F32 = mybir.dt.float32
BF16 = mybir.dt.bfloat16
I32 = mybir.dt.int32
AF = mybir.ActivationFunctionType
ALU = mybir.AluOpType
AX = mybir.AxisListType

D = 2048
T = 4096
NCD = 16
DQ = 682
NCQ = 6
QW = [128, 128, 128, 128, 128, 42]
DKV = 1088
NCKV = 9
KVW = [128] * 8 + [64]
NCKN = 8
H = 16
DH = 128
KVU = 3072
NCKVU = 24
HID = 5632
NCH = 44
SCALE = 1.0 / float(np.sqrt(128.0))
EPS = 1e-5

TA = 512
NTA = T // TA
NSUB = TA // 128
TB = 512
NTB = T // TB
TC = 512
NTC = T // TC
TD = 1024
NTD = T // TD

NGQ1 = NCQ
NG1 = NCQ + NCKV
NSQ1 = NCQ + NCKN
NGQ2 = NCD
NG2 = NCD + NCKVU

ALL_PHASES = ("A1", "A2", "B", "C", "G", "D")
BH = H
BFLASH = True
BCONV = True


def W(eng, semh, val):
    if val > 0:
        eng.wait_ge(semh, val)


class Chain:

    def __init__(self, eng, semh):
        self.eng, self.sem, self.n = eng, semh, 0

    def __call__(self, instr):
        self.n += 1
        instr.then_inc(self.sem, 1)
        self.eng.wait_ge(self.sem, self.n)
        return instr


def build_program(phases=ALL_PHASES, dbg=False, scr_in=(), dbg_b=False):
    nc = bass.Bass("TRN2", target_bir_lowering=False)
    dram = lambda name, shape, dt: nc.dram_tensor(name, shape, dt, kind="ExternalInput").ap()
    def scr(name, shape, dt):
        kind = "ExternalInput" if name in scr_in else ("ExternalOutput" if dbg else "Internal")
        return nc.dram_tensor(name, shape, dt, kind=kind).ap()

    env = {}
    env["xT"] = dram("xT", [D, T], F32)
    env["wdqT"] = dram("wdqT", [D, DQ], F32)
    env["wuqT"] = dram("wuqT", [DQ, D], F32)
    env["wdkvT"] = dram("wdkvT", [D, DKV], F32)
    env["wukvT"] = dram("wukvT", [1024, KVU], F32)
    env["woT"] = dram("woT", [D, D], F32)
    env["wgT"] = dram("wgT", [D, HID], I32)
    env["wuT2"] = dram("wuT2", [D, HID], I32)
    env["wdT"] = dram("wdT", [HID, D], I32)
    for nm, ncol in (("vdq", NCQ), ("vqg", NCQ), ("vqb", NCQ), ("vuq", NCD),
                     ("vdkv", NCKV), ("vkvg", NCKN), ("vkvb", NCKN),
                     ("vukv", NCKVU), ("vo", NCD), ("vsg", NCH), ("vsu", NCH),
                     ("vsd", NCD)):
        env[nm] = dram(nm, [128, ncol], F32)
    env["c_ident"] = dram("c_ident", [128, 128], F32)
    env["c_masks"] = dram("c_masks", [128, 4 * TB], F32)
    env["c_foldc"] = dram("c_foldc", [128, 512], F32)
    env["c_folds"] = dram("c_folds", [128, 512], F32)
    env["c_bfc"] = dram("c_bfc", [16, 32], F32)
    env["c_bfs"] = dram("c_bfs", [16, 32], F32)
    env["c_sel"] = dram("c_sel", [128, 64], F32)

    env["sc_cq"] = scr("sc_cq", [NCQ * 128, T], BF16)
    env["sc_kvn"] = scr("sc_kvn", [1024, T], BF16)
    env["sc_kr"] = scr("sc_kr", [64, T], BF16)
    env["sc_q"] = scr("sc_q", [D, T], BF16)
    env["sc_kn"] = scr("sc_kn", [1024, T], BF16)
    env["sc_vt"] = scr("sc_vt", [H * T, DH], BF16)
    env["sc_y"] = scr("sc_y", [D, T], BF16)
    env["sc_x1f"] = scr("sc_x1f", [D, T], F32)
    env["sc_z"] = scr("sc_z", [HID, T], BF16)
    env["sc_wd"] = scr("sc_wd", [D, HID], BF16)
    env["sc_dbg"] = scr("sc_dbg", [128, 3 * TB + 16], F32)
    env["outT"] = nc.dram_tensor("outT", [D, T], F32, kind="ExternalOutput").ap()

    with ExitStack() as ctx:
        sem = lambda n: ctx.enter_context(nc.semaphore(n))
        sb = lambda n, s, dt: ctx.enter_context(nc.sbuf_tensor(n, s, dt)).ap()
        env["nc"] = nc
        env["sem"] = sem

        env["dbg_b"] = dbg_b
        phase_list = [p for p in ALL_PHASES if p in phases]
        bars = {p: sem("bar_" + p) for p in phase_list}
        clr = sem("clr")
        env["clr"] = clr
        env["clr_n"] = [0]
        env["psem"] = sem

        env["ident_bf"] = sb("ident_bf", [128, 128], BF16)
        env["ident_f"] = sb("ident_f", [128, 128], F32)
        env["ones_bf"] = sb("ones_bf", [128, 1], BF16)
        env["ones_f"] = sb("ones_f", [128, 1], F32)
        env["onesr_f"] = sb("onesr_f", [1, 128], F32)
        env["masks_bf"] = sb("masks_bf", [128, 4 * TB], BF16)
        env["sel_bf"] = sb("sel_bf", [128, 64], BF16)
        c_ldg = sem("c_ldg")
        c_lds = sem("c_lds")
        c_ini = sem("c_ini")
        env["c_ldg"] = c_ldg
        env["c_lds"] = c_lds
        env["c_ini"] = c_ini

        block = ctx.enter_context(nc.Block())
        env["block"] = block

        @block.gpsimd
        def _(g):
            g.dma_start(out=env["ident_bf"], in_=env["c_ident"]).then_inc(c_ldg, 16)
            g.dma_start(out=env["masks_bf"], in_=env["c_masks"]).then_inc(c_ldg, 16)
            g.dma_start(out=env["sel_bf"], in_=env["c_sel"]).then_inc(c_ldg, 16)

        @block.sync
        def _(s):
            s.dma_start(out=env["ident_f"], in_=env["c_ident"]).then_inc(c_lds, 16)

        @block.vector
        def _(v):
            v.memset(env["ones_bf"], 1.0)
            v.memset(env["ones_f"], 1.0)
            v.memset(env["onesr_f"], 1.0).then_inc(c_ini, 1)

        prev_bar = None
        for pi, p in enumerate(phase_list):
            with ExitStack() as pctx:
                phase_handles = []

                def psem(n, _pctx=pctx, _h=phase_handles):
                    h = _pctx.enter_context(nc.semaphore(n))
                    _h.append(h)
                    return h

                env["sem"] = psem
                globals()["emit_" + p](env, prev_bar, bars[p])
                prev_bar = bars[p]
                if pi < len(phase_list) - 1 and phase_handles:
                    @block.gpsimd
                    def _(g, _bar=prev_bar):
                        g.wait_ge(_bar, 5)
                        nc.clear_and_free_semaphores(phase_handles)
                        g.sem_inc(clr, 1)
                    env["clr_n"][0] += 1

    return nc


def _bs(eng, bar_in, env=None):
    if bar_in is not None:
        eng.wait_ge(bar_in, 5)
    if env is not None and env["clr_n"][0] > 0:
        eng.wait_ge(env["clr"], env["clr_n"][0])


def emit_A1(env, bar_in, bar_out):
    nc, block, sem = env["nc"], env["block"], env["sem"]
    ones_f, onesr_f, c_ini = env["ones_f"], env["onesr_f"], env["c_ini"]

    with ExitStack() as ctx:
        sbl = lambda n, s, dt: ctx.enter_context(nc.sbuf_tensor(n, s, dt)).ap()
        ps = lambda n, s: ctx.enter_context(nc.psum_tensor(n, s, F32)).ap()

        wdq_sb = sbl("a1_wdq", [128, NCD * DQ], BF16)
        wdkv_sb = sbl("a1_wdkv", [128, NCD * DKV], BF16)
        v_dq = sbl("a1_vdq", [128, NCQ], F32)
        v_qg = sbl("a1_vqg", [128, NCQ], F32)
        v_qb = sbl("a1_vqb", [128, NCQ], F32)
        v_dkv = sbl("a1_vdkv", [128, NCKV], F32)
        v_kvg = sbl("a1_vkvg", [128, NCKN], F32)
        v_kvb = sbl("a1_vkvb", [128, NCKN], F32)
        x_bf = [sbl(f"a1_x{k}", [128, NCD * TA], BF16) for k in range(2)]
        cqf = sbl("a1_cqf", [128, NCQ * TA], BF16)
        kvf = sbl("a1_kvf", [128, NCKN * TA], BF16)
        sq_scr = [sbl(f"a1_sq{k}", [128, TA], BF16) for k in range(2)]
        tmp_n = [sbl(f"a1_tmp{k}", [128, TA], F32) for k in range(2)]
        cq_st = [sbl(f"a1_cqst{k}", [128, NCQ * TA], BF16) for k in range(2)]
        kv_st = [sbl(f"a1_kvst{k}", [128, NCKN * TA], BF16) for k in range(2)]
        kr_st = [sbl(f"a1_krst{k}", [64, TA], BF16) for k in range(2)]
        rows = {}
        for pref in ("q", "k"):
            for rn in ("mu", "t1", "a", "b2"):
                rows[pref + rn] = sbl(f"a1_r_{pref}{rn}", [1, TA], F32)

        proj_ps = [ps(f"a1_pp{k}", [128, TA]) for k in range(2)]
        Sq_ps = ps("a1_Sq", [1, TA])
        SSq_ps = ps("a1_SSq", [1, TA])
        Skv_ps = ps("a1_Skv", [1, TA])
        SSkv_ps = ps("a1_SSkv", [1, TA])
        bca_ps = ps("a1_bca", [128, TA])
        bcb_ps = ps("a1_bcb", [128, TA])

        s_ldw = env["psem"]("a1_ldw")
        s_ldv = sem("a1_ldv")
        s_ldx = [env["psem"]("a1_ldx0"), env["psem"]("a1_ldx1")]
        s_proj = sem("a1_proj")
        s_evac = sem("a1_evac")
        s_sq = sem("a1_sq")
        s_stat = sem("a1_stat")
        s_rv = sem("a1_rv")
        s_ra = sem("a1_ra")
        s_rows = sem("a1_rows")
        s_bc = sem("a1_bc")
        s_norm = sem("a1_norm")
        ch_v1 = sem("a1_chv")
        ch_s1 = sem("a1_chs")
        st_cq = [sem("a1_stcq0"), sem("a1_stcq1")]
        st_kr = [sem("a1_stkr0"), sem("a1_stkr1")]
        st_kv = [sem("a1_stkv0"), sem("a1_stkv1")]

        groups = [("q", ci, QW[ci]) for ci in range(NCQ)] + \
                 [("kv", ci, KVW[ci]) for ci in range(NCKV)]
        sqchunks = [("q", ci, QW[ci]) for ci in range(NCQ)] + \
                   [("kv", ci, 128) for ci in range(NCKN)]

        @block.gpsimd
        def _(g):
            _bs(g, bar_in, env)
            g.dma_start(out=wdq_sb.rearrange("p (c i) -> p c i", c=NCD),
                        in_=env["wdqT"].rearrange("(c p) i -> p c i", p=128)).then_inc(s_ldw, 16)
            g.dma_start(out=wdkv_sb.rearrange("p (c i) -> p c i", c=NCD),
                        in_=env["wdkvT"].rearrange("(c p) i -> p c i", p=128)).then_inc(s_ldw, 16)
            for i in range(NTA):
                pa = i % 2
                W(g, s_proj, NG1 * (i - 1))
                g.dma_start(
                    out=x_bf[pa].rearrange("p (c t) -> p c t", c=NCD),
                    in_=env["xT"].rearrange("(c p) t -> p c t", p=128)[:, :, i * TA:(i + 1) * TA],
                ).then_inc(s_ldx[pa], 16)
            g.sem_inc(bar_out, 1)

        @block.sync
        def _(s):
            _bs(s, bar_in, env)
            for dn, sn in (("a1v1", "vdq"), ("a1v2", "vqg"), ("a1v3", "vqb"),
                           ("a1v4", "vdkv"), ("a1v5", "vkvg"), ("a1v6", "vkvb")):
                pass
            for dst, src in ((v_dq, env["vdq"]), (v_qg, env["vqg"]), (v_qb, env["vqb"]),
                             (v_dkv, env["vdkv"]), (v_kvg, env["vkvg"]), (v_kvb, env["vkvb"])):
                s.dma_start(out=dst, in_=src).then_inc(s_ldv, 16)
            for i in range(NTA):
                _a1_spills(env, s, i, s_norm, s_evac,
                           st_cq, st_kr, st_kv, cq_st, kv_st, kr_st)
            for sm in (*st_cq, *st_kr, *st_kv):
                s.wait_ge(sm, 16 * (NTA // 2))
            s.sem_inc(bar_out, 1)

        @block.tensor
        def _(t):
            _bs(t, bar_in, env)
            t.wait_ge(s_ldw, 32)
            t.wait_ge(c_ini := env["c_ini"], 1)
            for i in range(NTA):
                pa = i % 2
                t.wait_ge(s_ldx[pa], 16 * (i // 2 + 1))
                for gi, (kind, ci, w) in enumerate(groups):
                    gg = NG1 * i + gi
                    W(t, s_evac, gg - 1)
                    wsb = wdq_sb if kind == "q" else wdkv_sb
                    wd_ = DQ if kind == "q" else DKV
                    for cd in range(NCD):
                        mi = t.matmul(
                            proj_ps[gi % 2][0:w, :],
                            wsb[:, cd * wd_ + ci * 128: cd * wd_ + ci * 128 + w],
                            x_bf[pa][:, cd * TA:(cd + 1) * TA],
                            start=(cd == 0), stop=(cd == NCD - 1))
                    mi.then_inc(s_proj, 1)
                    if gi >= 1:
                        _a1_stats(t, env, i, gi - 1, s_evac, s_sq, s_stat, s_rows,
                                  cqf, kvf, sq_scr, Sq_ps, SSq_ps, Skv_ps, SSkv_ps,
                                  ones_f)
                _a1_stats(t, env, i, NG1 - 1, s_evac, s_sq, s_stat, s_rows,
                          cqf, kvf, sq_scr, Sq_ps, SSq_ps, Skv_ps, SSkv_ps, ones_f)
                for ri, pref in enumerate(("q", "k")):
                    W(t, s_rows, 2 * i + ri + 1)
                    W(t, s_norm, NSQ1 * i if ri == 0 else NSQ1 * i + NCQ)
                    t.matmul(bca_ps, onesr_f, rows[pref + "a"], start=True, stop=True)
                    t.matmul(bcb_ps, onesr_f, rows[pref + "b2"],
                             start=True, stop=True).then_inc(s_bc, 1)
            t.wait_ge(s_bc, 2 * NTA)
            t.sem_inc(bar_out, 1)

        @block.scalar
        def _(sc):
            _bs(sc, bar_in, env)
            cs = Chain(sc, ch_s1)
            sc.wait_ge(s_ldv, 16 * 6)
            for i in range(NTA):
                pa = i % 2
                for gi, (kind, ci, w) in enumerate(groups):
                    gg = NG1 * i + gi
                    sc.wait_ge(s_proj, gg + 1)
                    if kind == "q":
                        W(sc, s_norm, NSQ1 * (i - 1) + ci + 1)
                        sc.activation(cqf[0:w, ci * TA:(ci + 1) * TA],
                                      proj_ps[gi % 2][0:w, :], AF.Identity,
                                      bias=v_dq[0:w, ci:ci + 1]).then_inc(s_evac, 1)
                    elif ci < NCKN:
                        W(sc, s_norm, NSQ1 * (i - 1) + NCQ + ci + 1)
                        sc.activation(kvf[0:w, ci * TA:(ci + 1) * TA],
                                      proj_ps[gi % 2][0:w, :], AF.Identity,
                                      bias=v_dkv[0:w, ci:ci + 1]).then_inc(s_evac, 1)
                    else:
                        W(sc, st_kr[pa], 16 * (i // 2))
                        sc.activation(kr_st[pa], proj_ps[gi % 2][0:w, :], AF.Identity,
                                      bias=v_dkv[0:w, ci:ci + 1]).then_inc(s_evac, 1)
                    if kind == "q" and ci == NCQ - 1:
                        _a1_rows_act(sc, cs, rows, "q", s_rv, s_ra, 2 * i + 1)
                _a1_rows_act(sc, cs, rows, "k", s_rv, s_ra, 2 * i + 2)
            sc.wait_ge(s_ra, 2 * NTA)
            sc.sem_inc(bar_out, 1)

        @block.vector
        def _(v):
            _bs(v, bar_in, env)
            cv = Chain(v, ch_v1)
            v.wait_ge(env["c_ini"], 1)
            for i in range(NTA):
                pa = i % 2
                for si, (kind, ci, w) in enumerate(sqchunks):
                    sg = NSQ1 * i + si
                    gi = ci if kind == "q" else NCQ + ci
                    v.wait_ge(s_evac, NG1 * i + gi + 1)
                    W(v, s_stat, sg - 1)
                    src = cqf if kind == "q" else kvf
                    v.tensor_mul(sq_scr[sg % 2][0:w, :],
                                 src[0:w, ci * TA:(ci + 1) * TA],
                                 src[0:w, ci * TA:(ci + 1) * TA]).then_inc(s_sq, 1)
                    if kind == "q" and ci == NCQ - 1:
                        _a1_rows_dve(v, cv, rows, "q", Sq_ps, SSq_ps, DQ,
                                     s_stat, NSQ1 * i + NCQ, s_ra, 2 * i + 1,
                                     s_rv, s_rows, s_bc, 2 * i - 1)
                    if kind == "kv" and ci == NCKN - 1:
                        _a1_rows_dve(v, cv, rows, "k", Skv_ps, SSkv_ps, 1024,
                                     s_stat, NSQ1 * i + NSQ1, s_ra, 2 * i + 2,
                                     s_rv, s_rows, s_bc, 2 * i)
                for si, (kind, ci, w) in enumerate(sqchunks):
                    bcidx = 2 * i + (1 if kind == "q" else 2)
                    v.wait_ge(s_bc, bcidx)
                    W(v, (st_cq if kind == "q" else st_kv)[pa], 16 * (i // 2))
                    src = cqf if kind == "q" else kvf
                    dst = cq_st if kind == "q" else kv_st
                    vg = v_qg if kind == "q" else v_kvg
                    vb = v_qb if kind == "q" else v_kvb
                    tp = tmp_n[si % 2]
                    cv(v.tensor_mul(tp[0:w, :], src[0:w, ci * TA:(ci + 1) * TA],
                                    bca_ps[0:w, :]))
                    cv(v.tensor_add(tp[0:w, :], tp[0:w, :], bcb_ps[0:w, :]))
                    v.tensor_scalar(dst[pa][0:w, ci * TA:(ci + 1) * TA], tp[0:w, :],
                                    vg[0:w, ci:ci + 1], vb[0:w, ci:ci + 1],
                                    ALU.mult, ALU.add).then_inc(s_norm, 1)
            v.wait_ge(s_norm, NSQ1 * NTA)
            v.sem_inc(bar_out, 1)


def _a1_spills(env, g, i, s_norm, s_evac, st_cq, st_kr, st_kv, cq_st, kv_st, kr_st):
    pa = i % 2
    W(g, s_norm, NSQ1 * i + NCQ)
    g.dma_start(
        out=env["sc_cq"].rearrange("(c p) t -> p c t", p=128)[:, :, i * TA:(i + 1) * TA],
        in_=cq_st[pa].rearrange("p (c t) -> p c t", c=NCQ),
    ).then_inc(st_cq[pa], 16)
    W(g, s_evac, NG1 * i + NG1)
    g.dma_start(out=env["sc_kr"][:, i * TA:(i + 1) * TA],
                in_=kr_st[pa]).then_inc(st_kr[pa], 16)
    W(g, s_norm, NSQ1 * i + NSQ1)
    g.dma_start(
        out=env["sc_kvn"].rearrange("(c p) t -> p c t", p=128)[:, :, i * TA:(i + 1) * TA],
        in_=kv_st[pa].rearrange("p (c t) -> p c t", c=NCKN),
    ).then_inc(st_kv[pa], 16)


def _a1_stats(t, env, i, gi, s_evac, s_sq, s_stat, s_rows,
              cqf, kvf, sq_scr, Sq_ps, SSq_ps, Skv_ps, SSkv_ps, ones_f):
    if gi < NCQ:
        kind, ci, w, si = "q", gi, QW[gi], gi
        S, SS, src, nch = Sq_ps, SSq_ps, cqf, NCQ
    elif gi < NCQ + NCKN:
        kind, ci, w, si = "kv", gi - NCQ, 128, gi
        S, SS, src, nch = Skv_ps, SSkv_ps, kvf, NCKN
    else:
        return
    first = (ci == 0)
    last = (ci == nch - 1)
    sg = NSQ1 * i + si
    t.wait_ge(s_sq, sg + 1)
    if first:
        W(t, s_rows, 2 * (i - 1) + (1 if kind == "q" else 2))
    t.matmul(S, env["ones_bf"][0:w, :], src[0:w, ci * TA:(ci + 1) * TA],
             start=first, stop=last)
    t.matmul(SS, env["ones_bf"][0:w, :], sq_scr[sg % 2][0:w, :],
             start=first, stop=last).then_inc(s_stat, 1)


def _a1_rows_dve(v, cv, rows, pref, S_ps, SS_ps, dim, s_stat, statv, s_ra, rav,
                 s_rv, s_rows, s_bc, bc_war):
    mu, t1, a, b2 = rows[pref + "mu"], rows[pref + "t1"], rows[pref + "a"], rows[pref + "b2"]
    v.wait_ge(s_stat, statv)
    W(v, s_bc, bc_war)
    inv = 1.0 / float(dim)
    cv(v.tensor_scalar_mul(mu, S_ps, inv))
    cv(v.tensor_scalar_mul(t1, SS_ps, inv))
    cv(v.tensor_mul(a, mu, mu))
    cv(v.tensor_sub(t1, t1, a))
    v.tensor_scalar_add(t1, t1, EPS).then_inc(s_rv, 1)
    v.wait_ge(s_ra, rav)
    cv(v.tensor_mul(b2, mu, a))
    v.tensor_scalar_mul(b2, b2, -1.0).then_inc(s_rows, 1)


def _a1_rows_act(sc, cs, rows, pref, s_rv, s_ra, rav):
    t1, a = rows[pref + "t1"], rows[pref + "a"]
    sc.wait_ge(s_rv, rav)
    cs(sc.activation(a, t1, AF.Ln))
    sc.activation(a, a, AF.Exp, scale=-0.5).then_inc(s_ra, 1)


def emit_A2(env, bar_in, bar_out):
    nc, block, sem = env["nc"], env["block"], env["sem"]
    ident_bf, ident_f = env["ident_bf"], env["ident_f"]
    c_ldg, c_lds = env["c_ldg"], env["c_lds"]

    with ExitStack() as ctx:
        sbl = lambda n, s, dt: ctx.enter_context(nc.sbuf_tensor(n, s, dt)).ap()
        ps = lambda n, s, dt=F32: ctx.enter_context(nc.psum_tensor(n, s, dt)).ap()

        wuq_bf = sbl("a2_wuq", [128, NCQ * D], BF16)
        wukv_bf = sbl("a2_wukv", [128, NCKN * KVU], BF16)
        v_uq = sbl("a2_vuq", [128, NCD], F32)
        v_ukv = sbl("a2_vukv", [128, NCKVU], F32)
        vb_t = sbl("a2_vbt", [16, 128], F32)
        bf16c = sbl("a2_bf16c", [16, 32], F32)
        bf16s = sbl("a2_bf16s", [16, 32], F32)
        bf_t1 = sbl("a2_bft1", [16, 32], F32)
        bf_t2 = sbl("a2_bft2", [16, 32], F32)
        cq_bf = [sbl(f"a2_cq{k}", [128, NCQ * TA], BF16) for k in range(2)]
        kvn_bf = [sbl(f"a2_kvn{k}", [128, NCKN * TA], BF16) for k in range(2)]
        q_st = [sbl(f"a2_qst{k}", [128, NCD * TA], BF16) for k in range(2)]
        kn_st = [sbl(f"a2_knst{k}", [128, NCKN * TA], BF16) for k in range(2)]
        vt_only = sbl("a2_vtst", [128, H * TA], BF16)
        vt_st = [vt_only, vt_only]
        v_ev = [sbl(f"a2_vev{k}", [128, TA], BF16) for k in range(2)]

        proj_ps = [ps(f"a2_pp{k}", [128, TA]) for k in range(3)]
        tr_ps = [ps(f"a2_tr{k}", [128, 128], BF16) for k in range(2)]
        bf_ps = ps("a2_bfps", [128, 128])

        s_ldw = env["psem"]("a2_ldw")
        s_ldf = sem("a2_ldf")
        s_ldq = [sem("a2_ldq0"), sem("a2_ldq1")]
        s_fold = sem("a2_fold")
        s_bftr = sem("a2_bftr")
        s_bfev = sem("a2_bfev")
        s_ldt = [sem("a2_ldt0"), sem("a2_ldt1")]
        s_proj = sem("a2_proj")
        s_evac = sem("a2_evac")
        s_tr = sem("a2_tr")
        s_trev = sem("a2_trev")
        ch_v2 = sem("a2_chv")
        st_q = [env["psem"]("a2_stq0"), env["psem"]("a2_stq1")]
        st_kn = [env["psem"]("a2_stkn0"), env["psem"]("a2_stkn1")]
        st_vt = [env["psem"]("a2_stvt0"), env["psem"]("a2_stvt1")]

        NFOLD = NCQ + 1

        with ExitStack() as fctx:
            fsb = lambda n, s, dt: fctx.enter_context(nc.sbuf_tensor(n, s, dt)).ap()
            wuqf_c = [fsb(f"a2_wuqf{k}", [128, D], F32) for k in range(2)]
            fold_c = fsb("a2_foldc", [128, 512], F32)
            fold_s = fsb("a2_folds", [128, 512], F32)
            fold_t1 = fsb("a2_ft1", [128, 512], F32)
            fold_t2 = fsb("a2_ft2", [128, 512], F32)

            @block.sync
            def _(s):
                _bs(s, bar_in, env)
                s.dma_start(out=fold_c, in_=env["c_foldc"]).then_inc(s_ldf, 16)
                s.dma_start(out=fold_s, in_=env["c_folds"]).then_inc(s_ldf, 16)
                s.dma_start(out=v_uq, in_=env["vuq"]).then_inc(s_ldf, 16)
                s.dma_start(out=v_ukv, in_=env["vukv"]).then_inc(s_ldf, 16)
                s.dma_start(out=bf16c, in_=env["c_bfc"]).then_inc(s_ldf, 16)
                s.dma_start(out=bf16s, in_=env["c_bfs"]).then_inc(s_ldf, 16)
                for ci in range(NCQ):
                    W(s, s_fold, ci - 1)
                    s.dma_start(out=wuqf_c[ci % 2][0:QW[ci], :],
                                in_=env["wuqT"][ci * 128: ci * 128 + QW[ci], :]
                                ).then_inc(s_ldq[ci % 2], 16)

            @block.tensor
            def _(t):
                _bs(t, bar_in, env)
                t.wait_ge(s_ldf, 16 * 6)
                t.wait_ge(c_ldg, 48)
                t.wait_ge(c_lds, 16)
                t.matmul(bf_ps[0:16, :], v_uq, ident_f,
                         start=True, stop=True).then_inc(s_bftr, 1)
                t.wait_ge(s_fold, NFOLD)
                t.matmul(bf_ps[0:128, 0:16], vb_t, ident_f[0:16, 0:16],
                         start=True, stop=True).then_inc(s_bftr, 1)

            @block.scalar
            def _(sc):
                _bs(sc, bar_in, env)
                sc.wait_ge(s_bftr, 1)
                sc.activation(vb_t, bf_ps[0:16, :], AF.Copy).then_inc(s_bfev, 1)
                sc.wait_ge(s_bftr, 2)
                sc.activation(v_uq, bf_ps[0:128, 0:16], AF.Copy).then_inc(s_bfev, 1)

            @block.vector
            def _(v):
                _bs(v, bar_in, env)
                cv = Chain(v, ch_v2)
                v.wait_ge(s_ldf, 16 * 6)
                for ci in range(NCQ):
                    w = QW[ci]
                    v.wait_ge(s_ldq[ci % 2], 16 * (ci // 2 + 1))
                    wf = wuqf_c[ci % 2]
                    r4 = wf.rearrange("p (h g j) -> p h g j", h=H, g=4)
                    Ac = r4[0:w, :, 2, :]
                    Bc = r4[0:w, :, 3, :]
                    fc = fold_c.rearrange("p (h j) -> p h j", h=H)[0:w, :, :]
                    fs = fold_s.rearrange("p (h j) -> p h j", h=H)[0:w, :, :]
                    t1 = fold_t1.rearrange("p (h j) -> p h j", h=H)[0:w, :, :]
                    t2 = fold_t2.rearrange("p (h j) -> p h j", h=H)[0:w, :, :]
                    cv(v.tensor_mul(t1, Bc, fs))
                    cv(v.tensor_mul(t2, Ac, fs))
                    cv(v.tensor_mul(Ac, Ac, fc))
                    cv(v.tensor_add(Ac, Ac, t1))
                    cv(v.tensor_mul(Bc, Bc, fc))
                    cv(v.tensor_sub(Bc, Bc, t2))
                    v.tensor_copy(wuq_bf[0:w, ci * D:(ci + 1) * D],
                                  wf[0:w, :]).then_inc(s_fold, 1)
                v.wait_ge(s_bfev, 1)
                A = vb_t[:, 64:96]
                B = vb_t[:, 96:128]
                cv(v.tensor_mul(bf_t1, B, bf16s))
                cv(v.tensor_mul(bf_t2, A, bf16s))
                cv(v.tensor_mul(A, A, bf16c))
                cv(v.tensor_add(A, A, bf_t1))
                cv(v.tensor_mul(B, B, bf16c))
                v.tensor_sub(B, B, bf_t2).then_inc(s_fold, 1)
                v.wait_ge(s_bfev, 2)
                v.sem_inc(bar_out, 1)

        @block.gpsimd
        def _(g):
            _bs(g, bar_in, env)
            g.dma_start(out=wukv_bf.rearrange("p (c j) -> p c j", c=NCKN),
                        in_=env["wukvT"].rearrange("(c p) j -> p c j", p=128)).then_inc(s_ldw, 16)
            for i in range(NTA):
                pa = i % 2
                W(g, s_evac, NG2 * i + NGQ2)
                g.dma_start(
                    out=env["sc_q"].rearrange("(c p) t -> p c t", p=128)[:, :, i * TA:(i + 1) * TA],
                    in_=q_st[pa].rearrange("p (c t) -> p c t", c=NCD),
                ).then_inc(st_q[pa], 16)
                W(g, s_evac, NG2 * i + NGQ2 + NCKN)
                g.dma_start(
                    out=env["sc_kn"].rearrange("(c p) t -> p c t", p=128)[:, :, i * TA:(i + 1) * TA],
                    in_=kn_st[pa].rearrange("p (c t) -> p c t", c=NCKN),
                ).then_inc(st_kn[pa], 16)
                W(g, s_trev, 4 * H * (i + 1))
                for sub in range(NSUB):
                    g.dma_start(
                        out=env["sc_vt"].rearrange("(h tt sub p) d -> tt sub p h d",
                                                   h=H, tt=NTA, sub=NSUB)[i, sub],
                        in_=vt_st[pa].rearrange("p (h sub d) -> p h sub d",
                                                h=H, sub=NSUB)[:, :, sub, :],
                    ).then_inc(st_vt[pa], 16)
            for sm in (*st_q, *st_kn):
                g.wait_ge(sm, 16 * (NTA // 2))
            for sm in st_vt:
                g.wait_ge(sm, 16 * NSUB * (NTA // 2))
            g.sem_inc(bar_out, 1)

        @block.sync
        def _(s):
            for i in range(NTA):
                pa = i % 2
                W(s, s_proj, NG2 * (i - 1))
                s.dma_start(
                    out=cq_bf[pa].rearrange("p (c t) -> p c t", c=NCQ),
                    in_=env["sc_cq"].rearrange("(c p) t -> p c t", p=128)[:, :, i * TA:(i + 1) * TA],
                ).then_inc(s_ldt[pa], 16)
                s.dma_start(
                    out=kvn_bf[pa].rearrange("p (c t) -> p c t", c=NCKN),
                    in_=env["sc_kvn"].rearrange("(c p) t -> p c t", p=128)[:, :, i * TA:(i + 1) * TA],
                ).then_inc(s_ldt[pa], 16)
            s.sem_inc(bar_out, 1)

        @block.tensor
        def _(t):
            t.wait_ge(s_fold, NCQ)
            t.wait_ge(s_bfev, 2)
            t.wait_ge(s_ldw, 16)
            for i in range(NTA):
                pa = i % 2
                t.wait_ge(s_ldt[pa], 32 * (i // 2 + 1))
                for j in range(NGQ2):
                    gg = NG2 * i + j
                    W(t, s_evac, gg - 2)
                    for ci in range(NCQ):
                        w = QW[ci]
                        mi = t.matmul(proj_ps[gg % 3],
                                      wuq_bf[0:w, ci * D + j * 128: ci * D + (j + 1) * 128],
                                      cq_bf[pa][0:w, ci * TA:(ci + 1) * TA],
                                      start=(ci == 0), stop=(ci == NCQ - 1))
                    mi.then_inc(s_proj, 1)
                def emit_trs(t, h):
                    t.wait_ge(s_evac, NG2 * i + NGQ2 + NCKN + h + 1)
                    for sub in range(NSUB):
                        gs = NSUB * (H * i + h) + sub
                        W(t, s_trev, gs - 1)
                        t.transpose(tr_ps[gs % 2],
                                    v_ev[h % 2][:, sub * 128:(sub + 1) * 128],
                                    ident_bf).then_inc(s_tr, 1)

                for j in range(NCKVU):
                    gg = NG2 * i + NGQ2 + j
                    W(t, s_evac, gg - 2)
                    for ci in range(NCKN):
                        mi = t.matmul(proj_ps[gg % 3],
                                      wukv_bf[:, ci * KVU + j * 128: ci * KVU + (j + 1) * 128],
                                      kvn_bf[pa][:, ci * TA:(ci + 1) * TA],
                                      start=(ci == 0), stop=(ci == NCKN - 1))
                    mi.then_inc(s_proj, 1)
                    if j >= NCKN + 1:
                        emit_trs(t, j - NCKN - 1)
                emit_trs(t, H - 1)
            t.wait_ge(s_tr, NSUB * H * NTA)
            t.sem_inc(bar_out, 1)

        @block.scalar
        def _(sc):
            for i in range(NTA):
                pa = i % 2
                for j in range(NGQ2):
                    gg = NG2 * i + j
                    sc.wait_ge(s_proj, gg + 1)
                    W(sc, st_q[pa], 16 * (i // 2))
                    sc.activation(q_st[pa][:, j * TA:(j + 1) * TA], proj_ps[gg % 3],
                                  AF.Identity, bias=v_uq[:, j:j + 1]).then_inc(s_evac, 1)
                for j in range(NCKVU):
                    gg = NG2 * i + NGQ2 + j
                    sc.wait_ge(s_proj, gg + 1)
                    if j < NCKN:
                        W(sc, st_kn[pa], 16 * (i // 2))
                        sc.activation(kn_st[pa][:, j * TA:(j + 1) * TA], proj_ps[gg % 3],
                                      AF.Identity, bias=v_ukv[:, j:j + 1]).then_inc(s_evac, 1)
                    else:
                        h = j - NCKN
                        W(sc, s_tr, NSUB * (H * i + h - 1))
                        sc.activation(v_ev[h % 2], proj_ps[gg % 3],
                                      AF.Identity, bias=v_ukv[:, j:j + 1]).then_inc(s_evac, 1)
                        for sub in range(NSUB):
                            gs = NSUB * (H * i + h) + sub
                            sc.wait_ge(s_tr, gs + 1)
                            if i > 0:
                                W(sc, st_vt[(i - 1) % 2], 16 * NSUB * ((i - 1) // 2 + 1))
                            sc.activation(vt_st[pa][:, h * TA + sub * 128: h * TA + (sub + 1) * 128],
                                          tr_ps[gs % 2], AF.Copy).then_inc(s_trev, 1)
            sc.wait_ge(s_trev, NSUB * H * NTA)
            sc.sem_inc(bar_out, 1)


def emit_B(env, bar_in, bar_out):
    nc, block, sem = env["nc"], env["block"], env["sem"]
    ones_bf, onesr_f, masks_bf = env["ones_bf"], env["onesr_f"], env["masks_bf"]
    ones_f = env["ones_f"]

    visits = []
    for h in range(BH):
        for i in range(NTB):
            nck = 4 * (i + 1)
            for ck in range(nck):
                visits.append((h, i, ck, ck >= 4 * i, ck == nck - 1))
    NV = len(visits)
    VH = 144
    dcum = []
    dc = 0
    for (_, _, _, dg, _) in visits:
        dc += 1 if dg else 0
        dcum.append(dc)

    with ExitStack() as ctx:
        sbl = lambda n, s, dt: ctx.enter_context(nc.sbuf_tensor(n, s, dt)).ap()
        ps = lambda n, s, dt=F32: ctx.enter_context(nc.psum_tensor(n, s, dt)).ap()

        q_t = [sbl(f"b_q{k}", [128, T], BF16) for k in range(2)]
        k_t = [sbl(f"b_k{k}", [128, T], BF16) for k in range(2)]
        v_t = [sbl(f"b_v{k}", [128, T], BF16) for k in range(2)]
        p_sb = [sbl(f"b_p{k}", [128, TB], BF16) for k in range(3)]
        sq_scr = sbl("b_sq", [128, 16 * TB], BF16)
        redq8 = sbl("b_rq8", [8, 1], F32)
        redk8 = sbl("b_rk8", [8, 1], F32)
        l_t = sbl("b_lt", [1, TB], F32)
        nrm8q = sbl("b_n8q", [1, 8], F32)
        nrm8k = sbl("b_n8k", [1, 8], F32)
        nsc = sbl("b_nsc", [1, 8], F32)
        negB_row = sbl("b_negBr", [1, 1], F32)
        negB_sb = sbl("b_negB", [128, 1], F32)
        l_sb = sbl("b_l", [1, TB], F32)
        r_sb = sbl("b_r", [1, TB], F32)
        rbc_sb = sbl("b_rbc", [128, TB], F32)
        y_stage = [sbl(f"b_yst{k}", [128, T], BF16) for k in range(2)]
        conv_sb = [sbl(f"b_cv{k}", [128, D], BF16) for k in range(2)]

        s_ps = [ps(f"b_s{k}", [128, TB]) for k in range(3)]
        y_ps = [ps(f"b_y{k}", [128, TB]) for k in range(2)]
        l_ps = [ps(f"b_lp{k}", [8, TB]) for k in range(2)]
        rbc_ps = ps("b_rbcp", [128, TB])
        nb_ps = rbc_ps[:, 0:1]

        s_ldb = [sem("b_ldb0"), sem("b_ldb1")]
        s_s = sem("b_s")
        s_p = sem("b_p")
        s_pm = sem("b_pm")
        s_ly = sem("b_ly")
        s_lv = sem("b_lv")
        s_fr = sem("b_fr")
        s_bc = sem("b_bc")
        s_rbv = sem("b_rbv")
        s_yf = sem("b_yf")
        s_sq = sem("b_sq")
        s_nm = sem("b_nm")
        s_nr = sem("b_nr")
        s_np = sem("b_np")
        s_nle = sem("b_nle")
        s_nrow = sem("b_nrow")
        s_nbm = sem("b_nbm")
        s_nbe = sem("b_nbe")
        s_ns = sem("b_ns")
        ch_vb = sem("b_chv")
        ch_sb = sem("b_chs")
        s_sty = [env["psem"]("b_sty0"), env["psem"]("b_sty1")]
        s_ldc = [env["psem"]("b_ldc0"), env["psem"]("b_ldc1")]
        s_stc = [env["psem"]("b_stc0"), env["psem"]("b_stc1")]
        s_cva = sem("b_cva")

        @block.sync
        def _(s):
            _bs(s, bar_in, env)
            for h in range(BH):
                pa = h % 2
                W(s, s_ly, VH * (h - 1))
                s.dma_start(out=q_t[pa],
                            in_=env["sc_q"].rearrange("(c p) t -> p c t", p=128)[:, h, :]
                            ).then_inc(s_ldb[pa], 16)
                s.dma_start(out=k_t[pa][0:64, :],
                            in_=env["sc_kn"][h * 64:(h + 1) * 64, :]).then_inc(s_ldb[pa], 16)
                s.dma_start(out=k_t[pa][64:128, :],
                            in_=env["sc_kr"]).then_inc(s_ldb[pa], 16)
                s.dma_start(out=v_t[pa].rearrange("p (ck d) -> p ck d", ck=T // 128),
                            in_=env["sc_vt"].rearrange("(hh ck p) d -> hh p ck d",
                                                       hh=H, p=128)[h]).then_inc(s_ldb[pa], 16)
            s.sem_inc(bar_out, 1)

        @block.gpsimd
        def _(g):
            _bs(g, bar_in, env)
            cvj = [0]

            def conv_step(g):
                j = cvj[0]
                if j >= NCH or not BCONV:
                    return
                cvj[0] += 1
                W(g, s_stc[j % 2], 16 * (j // 2))
                g.dma_start(out=conv_sb[j % 2],
                            in_=env["wdT"][j * 128:(j + 1) * 128, :]
                            ).then_inc(s_ldc[j % 2], 16)
                g.wait_ge(s_cva, j + 1)
                g.dma_start(
                    out=env["sc_wd"].rearrange("(cb p) (hc m) -> p cb hc m",
                                               p=128, m=128)[:, :, j, :],
                    in_=conv_sb[j % 2].rearrange("p (cb m) -> p cb m", m=128),
                ).then_inc(s_stc[j % 2], 16)

            for h in range(BH):
                for _ in range(3):
                    conv_step(g)
                if BFLASH:
                    g.wait_ge(s_yf, NTB * (h + 1))
                    g.dma_start(out=env["sc_y"].rearrange("(c p) t -> p c t", p=128)[:, h, :],
                                in_=y_stage[h % 2]).then_inc(s_sty[h % 2], 16)
            while BCONV and cvj[0] < NCH:
                conv_step(g)
            if BCONV:
                g.wait_ge(s_stc[0], 16 * ((NCH + 1) // 2))
                g.wait_ge(s_stc[1], 16 * (NCH // 2))
            if BFLASH:
                W(g, s_sty[0], 16 * ((BH + 1) // 2))
                W(g, s_sty[1], 16 * (BH // 2))
            if env.get("dbg_b"):
                dbg = env["sc_dbg"]
                g.dma_start(out=dbg[0:1, 0:8], in_=nsc).then_inc(s_ldc[0], 16)
                with nc.allow_non_contiguous_dma(reason="debug dump"):
                    g.dma_start(out=dbg[:, 8:9], in_=negB_sb).then_inc(s_ldc[0], 16)
                g.dma_start(out=dbg[0:1, 16:16 + TB], in_=l_sb).then_inc(s_ldc[0], 16)
                g.dma_start(out=dbg[0:1, 16 + TB:16 + 2 * TB],
                            in_=r_sb).then_inc(s_ldc[0], 16)
                g.dma_start(out=dbg[0:8, 16 + 2 * TB:16 + 3 * TB],
                            in_=redq8).then_inc(s_ldc[0], 16) if False else None
                g.wait_ge(s_ldc[0], 16 * ((NCH + 1) // 2) * 0 + 16 * 4 + (16 * ((NCH + 1) // 2) if BCONV else 0))
            g.sem_inc(bar_out, 1)

        @block.tensor
        def _(t):
            _bs(t, bar_in, env)
            pend_bc = [None]

            def flush_bc(t):
                if pend_bc[0] is None:
                    return
                f = pend_bc[0]
                pend_bc[0] = None
                t.wait_ge(s_fr, f + 1)
                W(t, s_rbv, f)
                t.matmul(rbc_ps, onesr_f, r_sb, start=True, stop=True).then_inc(s_bc, 1)

            t.wait_ge(env["c_ini"], 1)
            for h in range(BH):
                pa = h % 2
                t.wait_ge(s_ldb[pa], 64 * (h // 2 + 1))
                W(t, s_lv, 8 * h)

                def emit_ly(t, vv):
                    _, i, ck, diag, last = visits[vv]
                    f = 8 * h + i
                    if diag:
                        t.wait_ge(s_pm, dcum[vv])
                    else:
                        t.wait_ge(s_p, vv + 1)
                    if ck == 0:
                        W(t, s_yf, f - 1)
                        W(t, s_lv, f - 1)
                    t.matmul(l_ps[i % 2][0:1, :], ones_bf, p_sb[vv % 3],
                             start=(ck == 0), stop=last)
                    t.matmul(y_ps[i % 2], v_t[pa][:, ck * 128:(ck + 1) * 128],
                             p_sb[vv % 3], start=(ck == 0), stop=last).then_inc(s_ly, 1)
                    if last:
                        pend_bc[0] = f

                for vv in (range(VH * h, VH * (h + 1)) if BFLASH else ()):
                    _, i, ck, diag, last = visits[vv]
                    if ck == 6:
                        flush_bc(t)
                    if vv < 3:
                        W(t, s_p, vv - 2)
                    if vv - 3 >= VH * h:
                        emit_ly(t, vv - 3)
                    t.matmul(s_ps[vv % 3], k_t[pa][:, ck * 128:(ck + 1) * 128],
                             q_t[pa][:, i * TB:(i + 1) * TB],
                             start=True, stop=True).then_inc(s_s, 1)
                if BFLASH:
                    emit_ly(t, VH * (h + 1) - 3)
                    emit_ly(t, VH * (h + 1) - 2)
                    emit_ly(t, VH * (h + 1) - 1)
                flush_bc(t)
            W(t, s_bc, 8 * BH if BFLASH else 0)
            t.sem_inc(bar_out, 1)

        @block.scalar
        def _(sc):
            _bs(sc, bar_in, env)
            csb = Chain(sc, ch_sb)
            for h in range(BH):

                def emit_r(sc, vv2):
                    f2 = 8 * h + visits[vv2][1]
                    sc.wait_ge(s_lv, f2 + 1)
                    csb(sc.activation(l_t, l_sb, AF.Ln))
                    W(sc, s_bc, f2)
                    sc.activation(r_sb, l_t, AF.Exp, scale=-1.0).then_inc(s_fr, 1)

                if h == 0:
                    sc.wait_ge(s_nrow, 1)
                for vv in (range(VH * h, VH * (h + 1)) if BFLASH else ()):
                    _, i, ck, diag, last = visits[vv]
                    sc.wait_ge(s_s, vv + 1)
                    W(sc, s_ly, vv - 2)
                    sc.activation(p_sb[vv % 3], s_ps[vv % 3], AF.Exp,
                                  bias=negB_sb, scale=SCALE).then_inc(s_p, 1)
                    if vv - 2 >= VH * h and visits[vv - 2][4]:
                        emit_r(sc, vv - 2)
                if BFLASH:
                    if visits[VH * (h + 1) - 2][4]:
                        emit_r(sc, VH * (h + 1) - 2)
                    emit_r(sc, VH * (h + 1) - 1)
            W(sc, s_fr, 8 * BH if BFLASH else 0)
            sc.sem_inc(bar_out, 1)

        @block.vector
        def _(v):
            _bs(v, bar_in, env)
            cvb = Chain(v, ch_vb)
            W(v, env["c_ldg"], 48)

            cvj2 = [0]

            def conv_ts(v):
                j = cvj2[0]
                if j >= NCH or not BCONV:
                    return
                cvj2[0] += 1
                W(v, s_stc[j % 2], 16 * (j // 2))
                v.wait_ge(s_ldc[j % 2], 16 * (j // 2 + 1))
                v.tensor_scalar_add(conv_sb[j % 2], conv_sb[j % 2],
                                    -128.0).then_inc(s_cva, 1)

            v.memset(negB_sb, -20.0).then_inc(s_nrow, 1)
            for h in range(BH):
                pa = h % 2
                for vv in (range(VH * h, VH * (h + 1)) if BFLASH else ()):
                    _, i, ck, diag, last = visits[vv]
                    f = 8 * h + i
                    if vv % 48 == 24:
                        conv_ts(v)
                    if diag:
                        v.wait_ge(s_p, vv + 1)
                        off = ck - 4 * i
                        v.tensor_mul(p_sb[vv % 3], p_sb[vv % 3],
                                     masks_bf[:, off * TB:(off + 1) * TB]
                                     ).then_inc(s_pm, 1)
                    if last:
                        v.wait_ge(s_ly, vv + 1)
                        v.tensor_copy(l_sb, l_ps[i % 2][0:1, :]).then_inc(s_lv, 1)
                        v.wait_ge(s_bc, f + 1)
                        v.tensor_copy(rbc_sb, rbc_ps).then_inc(s_rbv, 1)
                        v.wait_ge(s_rbv, f + 1)
                        if i == 0:
                            W(v, s_sty[h % 2], 16 * (h // 2))
                        v.tensor_mul(y_stage[h % 2][:, i * TB:(i + 1) * TB],
                                     y_ps[i % 2], rbc_sb).then_inc(s_yf, 1)
            while BCONV and cvj2[0] < NCH:
                conv_ts(v)
            W(v, s_yf, 8 * BH if BFLASH else 0)
            W(v, s_cva, NCH if BCONV else 0)
            v.sem_inc(bar_out, 1)


def emit_C(env, bar_in, bar_out):
    nc, block, sem = env["nc"], env["block"], env["sem"]
    ones_f, onesr_f = env["ones_f"], env["onesr_f"]

    with ExitStack() as ctx:
        sbl = lambda n, s, dt: ctx.enter_context(nc.sbuf_tensor(n, s, dt)).ap()
        ps = lambda n, s: ctx.enter_context(nc.psum_tensor(n, s, F32)).ap()

        wo_bf = sbl("c_wo", [128, NCD * D], BF16)
        v_o = sbl("c_vo", [128, NCD], F32)
        y_t = [sbl(f"c_y{k}", [128, NCD * TC], BF16) for k in range(2)]
        xu_t = [sbl(f"c_xu{k}", [128, NCD * TC], F32) for k in range(2)]
        z_sb = [sbl(f"c_z{k}", [128, TC], F32) for k in range(2)]
        sq_scr = [sbl(f"c_sq{k}", [128, TC], BF16) for k in range(2)]
        ub_scr = [sbl(f"c_ub{k}", [128, TC], BF16) for k in range(2)]
        rows = {}
        for rn in ("mu", "t1", "a", "b2"):
            rows["q" + rn] = sbl(f"c_r_{rn}", [1, TC], F32)

        z_ps = [ps(f"c_zp{k}", [128, TC]) for k in range(3)]
        S_ps = ps("c_S", [1, TC])
        SS_ps = ps("c_SS", [1, TC])
        bca_ps = ps("c_bca", [128, TC])
        bcb_ps = ps("c_bcb", [128, TC])
        rows["_S"] = S_ps
        rows["_SS"] = SS_ps

        c_ldw = env["psem"]("c_ldw")
        c_ldv = sem("c_ldv")
        c_ldt = [sem("c_ldt0"), sem("c_ldt1")]
        c_mm = sem("c_mm")
        c_ev = sem("c_ev")
        c_add = sem("c_add")
        c_sq = sem("c_sq")
        c_stat = sem("c_stat")
        c_rv = sem("c_rv")
        c_ra = sem("c_ra")
        c_rows = sem("c_rows")
        c_bc = sem("c_bc")
        c_norm = sem("c_norm")
        st_x1 = [env["psem"]("c_stx0"), env["psem"]("c_stx1")]
        ch_vc = sem("c_chv")
        ch_sc = sem("c_chs")

        @block.gpsimd
        def _(g):
            _bs(g, bar_in, env)
            g.dma_start(out=wo_bf.rearrange("p (c j) -> p c j", c=NCD),
                        in_=env["woT"].rearrange("(c p) j -> p c j", p=128)).then_inc(c_ldw, 16)
            for i in range(NTC):
                pa = i % 2
                W(g, c_norm, NCD * (i + 1))
                g.dma_start(
                    out=env["sc_x1f"].rearrange("(c p) t -> p c t", p=128)[:, :, i * TC:(i + 1) * TC],
                    in_=xu_t[pa].rearrange("p (c t) -> p c t", c=NCD),
                ).then_inc(st_x1[pa], 16)
            for sm in st_x1:
                g.wait_ge(sm, 16 * (NTC // 2))
            g.sem_inc(bar_out, 1)

        @block.sync
        def _(s):
            _bs(s, bar_in, env)
            s.dma_start(out=v_o, in_=env["vo"]).then_inc(c_ldv, 16)
            for i in range(NTC):
                pa = i % 2
                W(s, c_mm, NCD * (i - 1))
                W(s, st_x1[pa], 16 * (i // 2))
                s.dma_start(
                    out=y_t[pa].rearrange("p (c t) -> p c t", c=NCD),
                    in_=env["sc_y"].rearrange("(c p) t -> p c t", p=128)[:, :, i * TC:(i + 1) * TC],
                ).then_inc(c_ldt[pa], 16)
                s.dma_start(
                    out=xu_t[pa].rearrange("p (c t) -> p c t", c=NCD),
                    in_=env["xT"].rearrange("(c p) t -> p c t", p=128)[:, :, i * TC:(i + 1) * TC],
                ).then_inc(c_ldt[pa], 16)
            s.sem_inc(bar_out, 1)

        @block.tensor
        def _(t):
            _bs(t, bar_in, env)
            t.wait_ge(c_ldw, 16)
            t.wait_ge(env["c_ini"], 1)
            pend = [None]

            def flush_cbc(t):
                if pend[0] is None:
                    return
                ii = pend[0]
                pend[0] = None
                t.wait_ge(c_rows, ii + 1)
                W(t, c_norm, NCD * ii)
                t.matmul(bca_ps, onesr_f, rows["qa"], start=True, stop=True)
                t.matmul(bcb_ps, onesr_f, rows["qb2"],
                         start=True, stop=True).then_inc(c_bc, 1)

            for i in range(NTC):
                pa = i % 2
                t.wait_ge(c_ldt[pa], 32 * (i // 2 + 1))
                for c in range(NCD):
                    gg = NCD * i + c
                    W(t, c_ev, gg - 2)
                    for cd in range(NCD):
                        mi = t.matmul(z_ps[gg % 3],
                                      wo_bf[:, cd * D + c * 128: cd * D + (c + 1) * 128],
                                      y_t[pa][:, cd * TC:(cd + 1) * TC],
                                      start=(cd == 0), stop=(cd == NCD - 1))
                    mi.then_inc(c_mm, 1)
                    if c >= 1:
                        if c == 1:
                            flush_cbc(t)
                        _c_stats(t, env, i, c - 1, c_sq, c_stat, c_rows,
                                 ub_scr, sq_scr, S_ps, SS_ps, ones_f)
                _c_stats(t, env, i, NCD - 1, c_sq, c_stat, c_rows,
                         ub_scr, sq_scr, S_ps, SS_ps, ones_f)
                pend[0] = i
            flush_cbc(t)
            t.wait_ge(c_bc, NTC)
            t.sem_inc(bar_out, 1)

        @block.scalar
        def _(sc):
            _bs(sc, bar_in, env)
            cs = Chain(sc, ch_sc)
            sc.wait_ge(c_ldv, 16)
            for i in range(NTC):
                for c in range(NCD):
                    gg = NCD * i + c
                    sc.wait_ge(c_mm, gg + 1)
                    W(sc, c_add, gg - 1)
                    sc.activation(z_sb[gg % 2], z_ps[gg % 3], AF.Identity,
                                  bias=v_o[:, c:c + 1]).then_inc(c_ev, 1)
                _c_rows_act(sc, cs, rows, c_rv, c_ra, i + 1)
            sc.wait_ge(c_ra, NTC)
            sc.sem_inc(bar_out, 1)

        @block.vector
        def _(v):
            _bs(v, bar_in, env)
            cv = Chain(v, ch_vc)
            for i in range(NTC):
                pa = i % 2
                for c in range(NCD):
                    gg = NCD * i + c
                    v.wait_ge(c_ev, gg + 1)
                    v.wait_ge(c_ldt[pa], 32 * (i // 2 + 1))
                    xc = xu_t[pa][:, c * TC:(c + 1) * TC]
                    cv(v.tensor_add(xc, xc, z_sb[gg % 2]))
                    v.engine_nop().then_inc(c_add, 1)
                    W(v, c_stat, gg - 1)
                    v.tensor_copy(ub_scr[gg % 2], xc)
                    v.tensor_mul(sq_scr[gg % 2], xc, xc).then_inc(c_sq, 1)
                    if c == NCD - 1:
                        _c_rows_dve(v, cv, rows, 2048, c_stat, NCD * (i + 1),
                                    c_ra, i + 1, c_rv, c_rows, c_bc, i)
                for c in range(NCD):
                    v.wait_ge(c_bc, i + 1)
                    W(v, st_x1[pa], 16 * (i // 2))
                    xc = xu_t[pa][:, c * TC:(c + 1) * TC]
                    cv(v.tensor_mul(xc, xc, bca_ps))
                    v.tensor_add(xc, xc, bcb_ps).then_inc(c_norm, 1)
            v.wait_ge(c_norm, NCD * NTC)
            v.sem_inc(bar_out, 1)


def _c_stats(t, env, i, c, c_sq, c_stat, c_rows, ub, sq_scr, S_ps, SS_ps, ones_f):
    gg = NCD * i + c
    t.wait_ge(c_sq, gg + 1)
    if c == 0:
        W(t, c_rows, i)
    t.matmul(S_ps, env["ones_bf"], ub[gg % 2],
             start=(c == 0), stop=(c == NCD - 1))
    t.matmul(SS_ps, env["ones_bf"], sq_scr[gg % 2],
             start=(c == 0), stop=(c == NCD - 1)).then_inc(c_stat, 1)


def _c_rows_dve(v, cv, rows, dim, c_stat, statv, c_ra, rav, c_rv, c_rows,
                c_bc, bc_war):
    mu, t1, a, b2 = rows["qmu"], rows["qt1"], rows["qa"], rows["qb2"]
    v.wait_ge(c_stat, statv)
    W(v, c_bc, bc_war)
    S_ps, SS_ps = rows["_S"], rows["_SS"]
    inv = 1.0 / float(dim)
    cv(v.tensor_scalar_mul(mu, S_ps, inv))
    cv(v.tensor_scalar_mul(t1, SS_ps, inv))
    cv(v.tensor_mul(a, mu, mu))
    cv(v.tensor_sub(t1, t1, a))
    v.tensor_scalar_add(t1, t1, EPS).then_inc(c_rv, 1)
    v.wait_ge(c_ra, rav)
    cv(v.tensor_mul(b2, mu, a))
    v.tensor_scalar_mul(b2, b2, -1.0).then_inc(c_rows, 1)


def _c_rows_act(sc, cs, rows, c_rv, c_ra, rav):
    t1, a = rows["qt1"], rows["qa"]
    sc.wait_ge(c_rv, rav)
    cs(sc.activation(a, t1, AF.Ln))
    sc.activation(a, a, AF.Exp, scale=-0.5).then_inc(c_ra, 1)


def emit_G(env, bar_in, bar_out):
    nc, block, sem = env["nc"], env["block"], env["sem"]

    with ExitStack() as ctx:
        sbl = lambda n, s, dt: ctx.enter_context(nc.sbuf_tensor(n, s, dt)).ap()
        ps = lambda n, s: ctx.enter_context(nc.psum_tensor(n, s, F32)).ap()

        x1_bf = sbl("g_x1", [128, NCD * T], BF16)
        wg_sb = [sbl(f"g_wg{k}", [128, NCD * 128], BF16) for k in range(2)]
        wu_sb = [sbl(f"g_wu{k}", [128, NCD * 128], BF16) for k in range(2)]
        v_sg = sbl("g_vsg", [128, NCH], F32)
        v_su = sbl("g_vsu", [128, NCH], F32)
        gate_sb = [sbl(f"g_gt{k}", [128, TC], BF16) for k in range(2)]
        up_sb = [sbl(f"g_up{k}", [128, TC], BF16) for k in range(2)]
        z_st = [sbl(f"g_zst{k}", [128, NTC * TC], BF16) for k in range(2)]

        g_ps = [ps(f"g_g{k}", [128, TC]) for k in range(2)]
        u_ps = [ps(f"g_u{k}", [128, TC]) for k in range(2)]

        g_ldx = env["psem"]("g_ldx")
        g_ldv = sem("g_ldv")
        g_ldw = [env["psem"]("g_ldw0"), env["psem"]("g_ldw1")]
        g_mm = sem("g_mm")
        g_ev = sem("g_ev")
        g_z = sem("g_z")
        st_z = [env["psem"]("g_stz0"), env["psem"]("g_stz1")]

        NK = NTC

        @block.gpsimd
        def _(g):
            _bs(g, bar_in, env)
            g.dma_start(out=x1_bf.rearrange("p (c t) -> p c t", c=NCD),
                        in_=env["sc_x1f"].rearrange("(c p) t -> p c t", p=128)
                        ).then_inc(g_ldx, 16)
            for h in range(NCH):
                pa = h % 2
                W(g, g_mm, NK * (h - 1))
                g.dma_start(out=wg_sb[pa].rearrange("p (c j) -> p c j", c=NCD),
                            in_=env["wgT"].rearrange("(c p) j -> p c j", p=128)[
                                :, :, h * 128:(h + 1) * 128]).then_inc(g_ldw[pa], 16)
                g.dma_start(out=wu_sb[pa].rearrange("p (c j) -> p c j", c=NCD),
                            in_=env["wuT2"].rearrange("(c p) j -> p c j", p=128)[
                                :, :, h * 128:(h + 1) * 128]).then_inc(g_ldw[pa], 16)
                if h >= 1:
                    _g_spill(env, g, h - 1, g_z, st_z, z_st)
            _g_spill(env, g, NCH - 1, g_z, st_z, z_st)
            for sm in st_z:
                g.wait_ge(sm, 16 * (NCH // 2))
            g.sem_inc(bar_out, 1)

        @block.sync
        def _(s):
            _bs(s, bar_in, env)
            s.dma_start(out=v_sg, in_=env["vsg"]).then_inc(g_ldv, 16)
            s.dma_start(out=v_su, in_=env["vsu"]).then_inc(g_ldv, 16)
            s.sem_inc(bar_out, 1)

        @block.tensor
        def _(t):
            _bs(t, bar_in, env)
            t.wait_ge(g_ldx, 16)
            for h in range(NCH):
                pa = h % 2
                t.wait_ge(g_ldw[pa], 32 * (h // 2 + 1))
                for i in range(NK):
                    k = NK * h + i
                    W(t, g_ev, k - 1)
                    for cd in range(NCD):
                        t.matmul(g_ps[k % 2],
                                 wg_sb[pa][:, cd * 128:(cd + 1) * 128],
                                 x1_bf[:, cd * T + i * TC: cd * T + (i + 1) * TC],
                                 start=(cd == 0), stop=(cd == NCD - 1))
                    for cd in range(NCD):
                        mi = t.matmul(u_ps[k % 2],
                                      wu_sb[pa][:, cd * 128:(cd + 1) * 128],
                                      x1_bf[:, cd * T + i * TC: cd * T + (i + 1) * TC],
                                      start=(cd == 0), stop=(cd == NCD - 1))
                    mi.then_inc(g_mm, 1)
            t.wait_ge(g_mm, NK * NCH)
            t.sem_inc(bar_out, 1)

        @block.scalar
        def _(sc):
            _bs(sc, bar_in, env)
            sc.wait_ge(g_ldv, 32)
            for h in range(NCH):
                for i in range(NK):
                    k = NK * h + i
                    sc.wait_ge(g_mm, k + 1)
                    W(sc, g_z, k - 1)
                    sc.activation(gate_sb[k % 2], g_ps[k % 2], AF.Silu,
                                  scale=v_sg[:, h:h + 1])
                    sc.activation(up_sb[k % 2], u_ps[k % 2], AF.Identity,
                                  scale=v_su[:, h:h + 1]).then_inc(g_ev, 1)
            sc.wait_ge(g_ev, NK * NCH)
            sc.sem_inc(bar_out, 1)

        @block.vector
        def _(v):
            _bs(v, bar_in, env)
            for h in range(NCH):
                pa = h % 2
                for i in range(NK):
                    k = NK * h + i
                    v.wait_ge(g_ev, k + 1)
                    if i == 0:
                        W(v, st_z[pa], 16 * (h // 2))
                    v.tensor_mul(z_st[pa][:, i * TC:(i + 1) * TC],
                                 gate_sb[k % 2], up_sb[k % 2]).then_inc(g_z, 1)
            v.wait_ge(g_z, NK * NCH)
            v.sem_inc(bar_out, 1)


def _g_spill(env, g, h, g_z, st_z, z_st):
    pa = h % 2
    W(g, g_z, NTC * (h + 1))
    g.dma_start(out=env["sc_z"][h * 128:(h + 1) * 128, :],
                in_=z_st[pa]).then_inc(st_z[pa], 16)


def emit_D(env, bar_in, bar_out):
    nc, block, sem = env["nc"], env["block"], env["sem"]
    ones_f, onesr_f = env["ones_f"], env["onesr_f"]
    TDL = TC
    NTDL = T // TDL

    with ExitStack() as ctx:
        sbl = lambda n, s, dt: ctx.enter_context(nc.sbuf_tensor(n, s, dt)).ap()
        ps = lambda n, s: ctx.enter_context(nc.psum_tensor(n, s, F32)).ap()

        z_sb = [sbl(f"d_z{k}", [128, NCH * TDL], BF16) for k in range(2)]
        wd_c = [sbl(f"d_wd{k}", [128, NCH * 128], BF16) for k in range(2)]
        xu_t = [sbl(f"d_xu{k}", [128, NCD * TDL], F32) for k in range(2)]
        d_sb = [sbl(f"d_d{k}", [128, TDL], F32) for k in range(2)]
        sq_scr = [sbl(f"d_sq{k}", [128, TDL], BF16) for k in range(2)]
        ub_scr = [sbl(f"d_ub{k}", [128, TDL], BF16) for k in range(2)]
        v_sd = sbl("d_vsd", [128, NCD], F32)
        rows = {}
        for rn in ("mu", "t1", "a", "b2"):
            rows["q" + rn] = sbl(f"d_r_{rn}", [1, TDL], F32)

        out_ps = [ps(f"d_o{k}", [128, TDL]) for k in range(2)]
        S_ps = ps("d_S", [1, TDL])
        SS_ps = ps("d_SS", [1, TDL])
        bca_ps = ps("d_bca", [128, TDL])
        bcb_ps = ps("d_bcb", [128, TDL])

        d_ldv = sem("d_ldv")
        d_ldz = [sem("d_ldz0"), sem("d_ldz1")]
        d_ldx = [sem("d_ldx0"), sem("d_ldx1")]
        d_ldw = [sem("d_ldw0"), sem("d_ldw1")]
        d_mm = sem("d_mm")
        d_ev = sem("d_ev")
        d_add = sem("d_add")
        d_sq = sem("d_sq")
        d_stat = sem("d_stat")
        d_rv = sem("d_rv")
        d_ra = sem("d_ra")
        d_rows = sem("d_rows")
        d_bc = sem("d_bc")
        d_norm = sem("d_norm")
        st_out = [env["psem"]("d_sto0"), env["psem"]("d_sto1")]
        ch_vd = sem("d_chv")
        ch_sd = sem("d_chs")

        rows["_S"] = S_ps
        rows["_SS"] = SS_ps

        @block.sync
        def _(s):
            _bs(s, bar_in, env)
            s.dma_start(out=v_sd, in_=env["vsd"]).then_inc(d_ldv, 16)
            for i in range(NTDL):
                pa = i % 2
                W(s, d_mm, NCD * (i - 1))
                s.dma_start(
                    out=z_sb[pa].rearrange("p (c t) -> p c t", c=NCH),
                    in_=env["sc_z"].rearrange("(c p) t -> p c t", p=128)[
                        :, :, i * TDL:(i + 1) * TDL]).then_inc(d_ldz[pa], 16)
                W(s, st_out[pa], 16 * (i // 2))
                s.dma_start(
                    out=xu_t[pa].rearrange("p (c t) -> p c t", c=NCD),
                    in_=env["sc_x1f"].rearrange("(c p) t -> p c t", p=128)[
                        :, :, i * TDL:(i + 1) * TDL]).then_inc(d_ldx[pa], 16)
                for c in range(NCD):
                    gg = NCD * i + c
                    W(s, d_mm, gg - 1)
                    s.dma_start(
                        out=wd_c[gg % 2],
                        in_=env["sc_wd"][c * 128:(c + 1) * 128, :],
                    ).then_inc(d_ldw[gg % 2], 16)
            s.sem_inc(bar_out, 1)

        @block.tensor
        def _(t):
            _bs(t, bar_in, env)
            t.wait_ge(env["c_ini"], 1)
            pend = [None]

            def flush_dbc(t):
                if pend[0] is None:
                    return
                ii = pend[0]
                pend[0] = None
                t.wait_ge(d_rows, ii + 1)
                W(t, d_norm, NCD * ii)
                t.matmul(bca_ps, onesr_f, rows["qa"], start=True, stop=True)
                t.matmul(bcb_ps, onesr_f, rows["qb2"],
                         start=True, stop=True).then_inc(d_bc, 1)

            for i in range(NTDL):
                pa = i % 2
                t.wait_ge(d_ldz[pa], 16 * (i // 2 + 1))
                for c in range(NCD):
                    gg = NCD * i + c
                    t.wait_ge(d_ldw[gg % 2], 16 * (gg // 2 + 1))
                    W(t, d_ev, gg - 1)
                    for hc in range(NCH):
                        mi = t.matmul(out_ps[gg % 2],
                                      wd_c[gg % 2][:, hc * 128:(hc + 1) * 128],
                                      z_sb[pa][:, hc * TDL:(hc + 1) * TDL],
                                      start=(hc == 0), stop=(hc == NCH - 1))
                    mi.then_inc(d_mm, 1)
                    if c >= 1:
                        if c == 1:
                            flush_dbc(t)
                        _d_stats(t, env, i, c - 1, d_sq, d_stat, d_rows,
                                 ub_scr, sq_scr, S_ps, SS_ps, ones_f)
                _d_stats(t, env, i, NCD - 1, d_sq, d_stat, d_rows,
                         ub_scr, sq_scr, S_ps, SS_ps, ones_f)
                pend[0] = i
            flush_dbc(t)
            t.wait_ge(d_bc, NTDL)
            t.sem_inc(bar_out, 1)

        @block.scalar
        def _(sc):
            _bs(sc, bar_in, env)
            cs = Chain(sc, ch_sd)
            sc.wait_ge(d_ldv, 16)
            for i in range(NTDL):
                for c in range(NCD):
                    gg = NCD * i + c
                    sc.wait_ge(d_mm, gg + 1)
                    W(sc, d_add, gg - 1)
                    sc.activation(d_sb[gg % 2], out_ps[gg % 2], AF.Identity,
                                  scale=v_sd[:, c:c + 1]).then_inc(d_ev, 1)
                _c_rows_act(sc, cs, rows, d_rv, d_ra, i + 1)
            sc.wait_ge(d_ra, NTDL)
            sc.sem_inc(bar_out, 1)

        @block.vector
        def _(v):
            _bs(v, bar_in, env)
            cv = Chain(v, ch_vd)
            for i in range(NTDL):
                pa = i % 2
                v.wait_ge(d_ldx[pa], 16 * (i // 2 + 1))
                for c in range(NCD):
                    gg = NCD * i + c
                    v.wait_ge(d_ev, gg + 1)
                    xc = xu_t[pa][:, c * TDL:(c + 1) * TDL]
                    cv(v.tensor_add(xc, xc, d_sb[gg % 2]))
                    v.engine_nop().then_inc(d_add, 1)
                    W(v, d_stat, gg - 1)
                    v.tensor_copy(ub_scr[gg % 2], xc)
                    v.tensor_mul(sq_scr[gg % 2], xc, xc).then_inc(d_sq, 1)
                    if c == NCD - 1:
                        _c_rows_dve(v, cv, rows, 2048, d_stat, NCD * (i + 1),
                                    d_ra, i + 1, d_rv, d_rows, d_bc, i)
                for c in range(NCD):
                    v.wait_ge(d_bc, i + 1)
                    W(v, st_out[pa], 16 * (i // 2))
                    xc = xu_t[pa][:, c * TDL:(c + 1) * TDL]
                    cv(v.tensor_mul(xc, xc, bca_ps))
                    v.tensor_add(xc, xc, bcb_ps).then_inc(d_norm, 1)
            v.wait_ge(d_norm, NCD * NTDL)
            v.sem_inc(bar_out, 1)

        @block.gpsimd
        def _(g):
            _bs(g, bar_in, env)
            for i in range(NTDL):
                pa = i % 2
                g.wait_ge(d_norm, NCD * (i + 1))
                g.dma_start(
                    out=env["outT"].rearrange("(c p) t -> p c t", p=128)[
                        :, :, i * TDL:(i + 1) * TDL],
                    in_=xu_t[pa].rearrange("p (c t) -> p c t", c=NCD),
                ).then_inc(st_out[pa], 16)
            for sm in st_out:
                g.wait_ge(sm, 16 * (NTDL // 2))
            g.sem_inc(bar_out, 1)


def _d_stats(t, env, i, c, d_sq, d_stat, d_rows, ub, sq_scr, S_ps, SS_ps, ones_f):
    gg = NCD * i + c
    t.wait_ge(d_sq, gg + 1)
    if c == 0:
        W(t, d_rows, i)
    t.matmul(S_ps, env["ones_bf"], ub[gg % 2],
             start=(c == 0), stop=(c == NCD - 1))
    t.matmul(SS_ps, env["ones_bf"], sq_scr[gg % 2],
             start=(c == 0), stop=(c == NCD - 1)).then_inc(d_stat, 1)


def _chunkvec(v, n):
    out = np.zeros((n, 128), np.float32)
    vv = np.asarray(v, np.float32)
    out.flat[: vv.size] = vv
    return np.ascontiguousarray(out.T)


def make_inputs(inp, b):
    x = np.asarray(inp["L0_attn_x"])
    d = {}
    d["xT"] = np.ascontiguousarray(np.asarray(x[b], np.float32).T)
    d["wdqT"] = np.ascontiguousarray(np.asarray(inp["L0_attn_W_d_q"], np.float32).T)
    d["wuqT"] = np.ascontiguousarray(np.asarray(inp["L0_attn_W_u_q"], np.float32).T)
    d["wdkvT"] = np.ascontiguousarray(np.asarray(inp["L0_attn_W_d_kv"], np.float32).T)
    perm = [h * 192 + j for h in range(H) for j in range(64)] + \
           [h * 192 + 64 + j for h in range(H) for j in range(128)]
    d["wukvT"] = np.ascontiguousarray(np.asarray(inp["L0_attn_W_u_kv"], np.float32).T[:, perm])
    d["woT"] = np.ascontiguousarray(np.asarray(inp["L0_attn_W_o"], np.float32).T)
    d["wgT"] = np.ascontiguousarray(np.asarray(inp["L0_ffn_wq_gate"], np.int32).T)
    d["wuT2"] = np.ascontiguousarray(np.asarray(inp["L0_ffn_wq_up"], np.int32).T)
    d["wdT"] = np.ascontiguousarray(np.asarray(inp["L0_ffn_wq_down"], np.int32).T)
    d["vdq"] = _chunkvec(inp["L0_attn_b_d_q"], NCQ)
    d["vqg"] = _chunkvec(inp["L0_attn_q_norm_g"], NCQ)
    d["vqb"] = _chunkvec(inp["L0_attn_q_norm_b"], NCQ)
    d["vuq"] = _chunkvec(inp["L0_attn_b_u_q"], NCD)
    d["vdkv"] = _chunkvec(inp["L0_attn_b_d_kv"], NCKV)
    d["vkvg"] = _chunkvec(inp["L0_attn_kv_norm_g"], NCKN)
    d["vkvb"] = _chunkvec(inp["L0_attn_kv_norm_b"], NCKN)
    d["vukv"] = _chunkvec(np.asarray(inp["L0_attn_b_u_kv"], np.float32)[perm], NCKVU)
    d["vo"] = _chunkvec(inp["L0_attn_b_o"], NCD)
    d["vsg"] = _chunkvec(inp["L0_ffn_s_gate"], NCH)
    d["vsu"] = _chunkvec(inp["L0_ffn_s_up"], NCH)
    d["vsd"] = _chunkvec(inp["L0_ffn_s_down"], NCD)
    d["c_ident"] = np.eye(128, dtype=np.float32)
    m = np.zeros((128, 4 * TB), np.float32)
    p = np.arange(128)[:, None]
    f = np.arange(TB)[None, :]
    for k in range(4):
        m[:, k * TB:(k + 1) * TB] = ((p + k * 128) <= f).astype(np.float32)
    d["c_masks"] = m
    jj = np.arange(32)
    invf = 10000.0 ** (-jj / 32.0)
    ang = np.arange(H)[:, None] * invf[None, :]
    cosW = np.cos(ang).astype(np.float32)
    sinW = np.sin(ang).astype(np.float32)
    d["c_foldc"] = np.ascontiguousarray(
        np.broadcast_to(cosW.reshape(1, 512), (128, 512)).astype(np.float32))
    d["c_folds"] = np.ascontiguousarray(
        np.broadcast_to(sinW.reshape(1, 512), (128, 512)).astype(np.float32))
    d["c_bfc"] = np.ascontiguousarray(cosW)
    d["c_bfs"] = np.ascontiguousarray(sinW)
    sel = np.zeros((128, 64), np.float32)
    for j in range(8):
        sel[:, 8 * j + j] = 1.0
    d["c_sel"] = sel
    return d


def kernel(**inp):
    nc = build_program()
    in_maps = [make_inputs(inp, b) for b in range(8)]
    res = run_bass_kernel_spmd(nc, in_maps, core_ids=list(range(8)))
    out = np.stack([np.asarray(r["outT"]).T for r in res.results], axis=0)
    return np.ascontiguousarray(out.astype(np.float32, copy=False))


# version 17
# speedup vs baseline: 1.0378x; 1.0378x over previous
import numpy as np
import ml_dtypes
from contextlib import ExitStack

import concourse.bass as bass
import concourse.mybir as mybir
from concourse.bass_utils import run_bass_kernel_spmd

F32 = mybir.dt.float32
BF16 = mybir.dt.bfloat16
I32 = mybir.dt.int32
AF = mybir.ActivationFunctionType
ALU = mybir.AluOpType
AX = mybir.AxisListType

D = 2048
T = 4096
NCD = 16
DQ = 682
NCQ = 6
QW = [128, 128, 128, 128, 128, 42]
DKV = 1088
NCKV = 9
KVW = [128] * 8 + [64]
NCKN = 8
H = 16
DH = 128
KVU = 3072
NCKVU = 24
HID = 5632
NCH = 44
SCALE = 1.0 / float(np.sqrt(128.0))
EPS = 1e-5

TA = 512
NTA = T // TA
NSUB = TA // 128
TB = 512
NTB = T // TB
TC = 512
NTC = T // TC
TD = 1024
NTD = T // TD

NGQ1 = NCQ
NG1 = NCQ + NCKV
NSQ1 = NCQ + NCKN
NGQ2 = NCD
NG2 = NCD + NCKVU

ALL_PHASES = ("A1", "A2", "B", "C", "G", "D")
BH = H
BFLASH = True
BCONV = True


def W(eng, semh, val):
    if val > 0:
        eng.wait_ge(semh, val)


class Chain:

    def __init__(self, eng, semh):
        self.eng, self.sem, self.n = eng, semh, 0

    def __call__(self, instr):
        self.n += 1
        instr.then_inc(self.sem, 1)
        self.eng.wait_ge(self.sem, self.n)
        return instr


def build_program(phases=ALL_PHASES, dbg=False, scr_in=(), dbg_b=False):
    nc = bass.Bass("TRN2", target_bir_lowering=False)
    dram = lambda name, shape, dt: nc.dram_tensor(name, shape, dt, kind="ExternalInput").ap()
    def scr(name, shape, dt):
        kind = "ExternalInput" if name in scr_in else ("ExternalOutput" if dbg else "Internal")
        return nc.dram_tensor(name, shape, dt, kind=kind).ap()

    env = {}
    env["xT"] = dram("xT", [D, T], F32)
    env["wdqT"] = dram("wdqT", [D, DQ], F32)
    env["wuqT"] = dram("wuqT", [DQ, D], F32)
    env["wdkvT"] = dram("wdkvT", [D, DKV], F32)
    env["wukvT"] = dram("wukvT", [1024, KVU], F32)
    env["woT"] = dram("woT", [D, D], F32)
    env["wgT"] = dram("wgT", [D, HID], I32)
    env["wuT2"] = dram("wuT2", [D, HID], I32)
    env["wdT"] = dram("wdT", [HID, D], I32)
    for nm, ncol in (("vdq", NCQ), ("vqg", NCQ), ("vqb", NCQ), ("vuq", NCD),
                     ("vdkv", NCKV), ("vkvg", NCKN), ("vkvb", NCKN),
                     ("vukv", NCKVU), ("vo", NCD), ("vsg", NCH), ("vsu", NCH),
                     ("vsd", NCD)):
        env[nm] = dram(nm, [128, ncol], F32)
    env["c_ident"] = dram("c_ident", [128, 128], F32)
    env["c_masks"] = dram("c_masks", [128, 4 * TB], F32)
    env["c_foldc"] = dram("c_foldc", [128, 512], F32)
    env["c_folds"] = dram("c_folds", [128, 512], F32)
    env["c_bfc"] = dram("c_bfc", [16, 32], F32)
    env["c_bfs"] = dram("c_bfs", [16, 32], F32)
    env["c_sel"] = dram("c_sel", [128, 64], F32)

    env["sc_cq"] = scr("sc_cq", [NCQ * 128, T], BF16)
    env["sc_kvn"] = scr("sc_kvn", [1024, T], BF16)
    env["sc_kr"] = scr("sc_kr", [64, T], BF16)
    env["sc_q"] = scr("sc_q", [D, T], BF16)
    env["sc_kn"] = scr("sc_kn", [1024, T], BF16)
    env["sc_vt"] = scr("sc_vt", [H * T, DH], BF16)
    env["sc_y"] = scr("sc_y", [D, T], BF16)
    env["sc_x1f"] = scr("sc_x1f", [D, T], F32)
    env["sc_z"] = scr("sc_z", [HID, T], BF16)
    env["sc_wd"] = scr("sc_wd", [D, HID], BF16)
    env["sc_dbg"] = scr("sc_dbg", [128, 3 * TB + 16], F32)
    env["outT"] = nc.dram_tensor("outT", [D, T], F32, kind="ExternalOutput").ap()

    with ExitStack() as ctx:
        sem = lambda n: ctx.enter_context(nc.semaphore(n))
        sb = lambda n, s, dt: ctx.enter_context(nc.sbuf_tensor(n, s, dt)).ap()
        env["nc"] = nc
        env["sem"] = sem

        env["dbg_b"] = dbg_b
        phase_list = [p for p in ALL_PHASES if p in phases]
        bars = {p: sem("bar_" + p) for p in phase_list}
        clr = sem("clr")
        env["clr"] = clr
        env["clr_n"] = [0]
        env["psem"] = sem

        env["ident_bf"] = sb("ident_bf", [128, 128], BF16)
        env["ident_f"] = sb("ident_f", [128, 128], F32)
        env["ones_bf"] = sb("ones_bf", [128, 1], BF16)
        env["ones_f"] = sb("ones_f", [128, 1], F32)
        env["onesr_f"] = sb("onesr_f", [1, 128], F32)
        env["masks_bf"] = sb("masks_bf", [128, 4 * TB], BF16)
        env["sel_bf"] = sb("sel_bf", [128, 64], BF16)
        c_ldg = sem("c_ldg")
        c_lds = sem("c_lds")
        c_ini = sem("c_ini")
        env["c_ldg"] = c_ldg
        env["c_lds"] = c_lds
        env["c_ini"] = c_ini

        block = ctx.enter_context(nc.Block())
        env["block"] = block

        @block.gpsimd
        def _(g):
            g.dma_start(out=env["ident_bf"], in_=env["c_ident"]).then_inc(c_ldg, 16)
            g.dma_start(out=env["masks_bf"], in_=env["c_masks"]).then_inc(c_ldg, 16)
            g.dma_start(out=env["sel_bf"], in_=env["c_sel"]).then_inc(c_ldg, 16)

        @block.sync
        def _(s):
            s.dma_start(out=env["ident_f"], in_=env["c_ident"]).then_inc(c_lds, 16)

        @block.vector
        def _(v):
            v.memset(env["ones_bf"], 1.0)
            v.memset(env["ones_f"], 1.0)
            v.memset(env["onesr_f"], 1.0).then_inc(c_ini, 1)

        prev_bar = None
        bc_ctx = ExitStack()
        env["c_ldw"] = sem("c_ldw")
        for pi, p in enumerate(phase_list):
            if (p == "B" and "C" in phase_list) or (p == "C" and "wo_bf" not in env):
                env["wo_bf"] = bc_ctx.enter_context(
                    nc.sbuf_tensor("c_wo", [128, NCD * D], BF16)).ap()
            with ExitStack() as pctx:
                phase_handles = []

                def psem(n, _pctx=pctx, _h=phase_handles):
                    h = _pctx.enter_context(nc.semaphore(n))
                    _h.append(h)
                    return h

                env["sem"] = psem
                globals()["emit_" + p](env, prev_bar, bars[p])
                prev_bar = bars[p]
                if p == "C":
                    bc_ctx.close()
                if pi < len(phase_list) - 1 and phase_handles:
                    @block.gpsimd
                    def _(g, _bar=prev_bar):
                        g.wait_ge(_bar, 5)
                        nc.clear_and_free_semaphores(phase_handles)
                        g.sem_inc(clr, 1)
                    env["clr_n"][0] += 1

    return nc


def _bs(eng, bar_in, env=None):
    if bar_in is not None:
        eng.wait_ge(bar_in, 5)
    if env is not None and env["clr_n"][0] > 0:
        eng.wait_ge(env["clr"], env["clr_n"][0])


def emit_A1(env, bar_in, bar_out):
    nc, block, sem = env["nc"], env["block"], env["sem"]
    ones_f, onesr_f, c_ini = env["ones_f"], env["onesr_f"], env["c_ini"]

    with ExitStack() as ctx:
        sbl = lambda n, s, dt: ctx.enter_context(nc.sbuf_tensor(n, s, dt)).ap()
        ps = lambda n, s: ctx.enter_context(nc.psum_tensor(n, s, F32)).ap()

        wdq_sb = sbl("a1_wdq", [128, NCD * DQ], BF16)
        wdkv_sb = sbl("a1_wdkv", [128, NCD * DKV], BF16)
        v_dq = sbl("a1_vdq", [128, NCQ], F32)
        v_qg = sbl("a1_vqg", [128, NCQ], F32)
        v_qb = sbl("a1_vqb", [128, NCQ], F32)
        v_dkv = sbl("a1_vdkv", [128, NCKV], F32)
        v_kvg = sbl("a1_vkvg", [128, NCKN], F32)
        v_kvb = sbl("a1_vkvb", [128, NCKN], F32)
        x_bf = [sbl(f"a1_x{k}", [128, NCD * TA], BF16) for k in range(2)]
        cqf = sbl("a1_cqf", [128, NCQ * TA], BF16)
        kvf = sbl("a1_kvf", [128, NCKN * TA], BF16)
        sq_scr = [sbl(f"a1_sq{k}", [128, TA], BF16) for k in range(2)]
        tmp_n = [sbl(f"a1_tmp{k}", [128, TA], F32) for k in range(2)]
        cq_st = [sbl(f"a1_cqst{k}", [128, NCQ * TA], BF16) for k in range(2)]
        kv_st = [sbl(f"a1_kvst{k}", [128, NCKN * TA], BF16) for k in range(2)]
        kr_st = [sbl(f"a1_krst{k}", [64, TA], BF16) for k in range(2)]
        rows = {}
        for pref in ("q", "k"):
            for rn in ("mu", "t1", "a", "b2"):
                rows[pref + rn] = sbl(f"a1_r_{pref}{rn}", [1, TA], F32)

        proj_ps = [ps(f"a1_pp{k}", [128, TA]) for k in range(2)]
        Sq_ps = ps("a1_Sq", [1, TA])
        SSq_ps = ps("a1_SSq", [1, TA])
        Skv_ps = ps("a1_Skv", [1, TA])
        SSkv_ps = ps("a1_SSkv", [1, TA])
        bca_ps = ps("a1_bca", [128, TA])
        bcb_ps = ps("a1_bcb", [128, TA])

        s_ldw = env["psem"]("a1_ldw")
        s_ldv = sem("a1_ldv")
        s_ldx = [env["psem"]("a1_ldx0"), env["psem"]("a1_ldx1")]
        s_proj = sem("a1_proj")
        s_evac = sem("a1_evac")
        s_sq = sem("a1_sq")
        s_stat = sem("a1_stat")
        s_rv = sem("a1_rv")
        s_ra = sem("a1_ra")
        s_rows = sem("a1_rows")
        s_bc = sem("a1_bc")
        s_norm = sem("a1_norm")
        ch_v1 = sem("a1_chv")
        ch_s1 = sem("a1_chs")
        st_cq = [sem("a1_stcq0"), sem("a1_stcq1")]
        st_kr = [sem("a1_stkr0"), sem("a1_stkr1")]
        st_kv = [sem("a1_stkv0"), sem("a1_stkv1")]

        groups = [("q", ci, QW[ci]) for ci in range(NCQ)] + \
                 [("kv", ci, KVW[ci]) for ci in range(NCKV)]
        sqchunks = [("q", ci, QW[ci]) for ci in range(NCQ)] + \
                   [("kv", ci, 128) for ci in range(NCKN)]

        @block.gpsimd
        def _(g):
            _bs(g, bar_in, env)
            g.dma_start(out=wdq_sb.rearrange("p (c i) -> p c i", c=NCD),
                        in_=env["wdqT"].rearrange("(c p) i -> p c i", p=128)).then_inc(s_ldw, 16)
            g.dma_start(out=wdkv_sb.rearrange("p (c i) -> p c i", c=NCD),
                        in_=env["wdkvT"].rearrange("(c p) i -> p c i", p=128)).then_inc(s_ldw, 16)
            for i in range(NTA):
                pa = i % 2
                W(g, s_proj, NG1 * (i - 1))
                g.dma_start(
                    out=x_bf[pa].rearrange("p (c t) -> p c t", c=NCD),
                    in_=env["xT"].rearrange("(c p) t -> p c t", p=128)[:, :, i * TA:(i + 1) * TA],
                ).then_inc(s_ldx[pa], 16)
            g.sem_inc(bar_out, 1)

        @block.sync
        def _(s):
            _bs(s, bar_in, env)
            for dn, sn in (("a1v1", "vdq"), ("a1v2", "vqg"), ("a1v3", "vqb"),
                           ("a1v4", "vdkv"), ("a1v5", "vkvg"), ("a1v6", "vkvb")):
                pass
            for dst, src in ((v_dq, env["vdq"]), (v_qg, env["vqg"]), (v_qb, env["vqb"]),
                             (v_dkv, env["vdkv"]), (v_kvg, env["vkvg"]), (v_kvb, env["vkvb"])):
                s.dma_start(out=dst, in_=src).then_inc(s_ldv, 16)
            for i in range(NTA):
                _a1_spills(env, s, i, s_norm, s_evac,
                           st_cq, st_kr, st_kv, cq_st, kv_st, kr_st)
            for sm in (*st_cq, *st_kr, *st_kv):
                s.wait_ge(sm, 16 * (NTA // 2))
            s.sem_inc(bar_out, 1)

        @block.tensor
        def _(t):
            _bs(t, bar_in, env)
            t.wait_ge(s_ldw, 32)
            t.wait_ge(c_ini := env["c_ini"], 1)
            for i in range(NTA):
                pa = i % 2
                t.wait_ge(s_ldx[pa], 16 * (i // 2 + 1))
                for gi, (kind, ci, w) in enumerate(groups):
                    gg = NG1 * i + gi
                    W(t, s_evac, gg - 1)
                    wsb = wdq_sb if kind == "q" else wdkv_sb
                    wd_ = DQ if kind == "q" else DKV
                    for cd in range(NCD):
                        mi = t.matmul(
                            proj_ps[gi % 2][0:w, :],
                            wsb[:, cd * wd_ + ci * 128: cd * wd_ + ci * 128 + w],
                            x_bf[pa][:, cd * TA:(cd + 1) * TA],
                            start=(cd == 0), stop=(cd == NCD - 1))
                    mi.then_inc(s_proj, 1)
                    if gi >= 1:
                        _a1_stats(t, env, i, gi - 1, s_evac, s_sq, s_stat, s_rows,
                                  cqf, kvf, sq_scr, Sq_ps, SSq_ps, Skv_ps, SSkv_ps,
                                  ones_f)
                _a1_stats(t, env, i, NG1 - 1, s_evac, s_sq, s_stat, s_rows,
                          cqf, kvf, sq_scr, Sq_ps, SSq_ps, Skv_ps, SSkv_ps, ones_f)
                for ri, pref in enumerate(("q", "k")):
                    W(t, s_rows, 2 * i + ri + 1)
                    W(t, s_norm, NSQ1 * i if ri == 0 else NSQ1 * i + NCQ)
                    t.matmul(bca_ps, onesr_f, rows[pref + "a"], start=True, stop=True)
                    t.matmul(bcb_ps, onesr_f, rows[pref + "b2"],
                             start=True, stop=True).then_inc(s_bc, 1)
            t.wait_ge(s_bc, 2 * NTA)
            t.sem_inc(bar_out, 1)

        @block.scalar
        def _(sc):
            _bs(sc, bar_in, env)
            cs = Chain(sc, ch_s1)
            sc.wait_ge(s_ldv, 16 * 6)
            for i in range(NTA):
                pa = i % 2
                for gi, (kind, ci, w) in enumerate(groups):
                    gg = NG1 * i + gi
                    sc.wait_ge(s_proj, gg + 1)
                    if kind == "q":
                        W(sc, s_norm, NSQ1 * (i - 1) + ci + 1)
                        sc.activation(cqf[0:w, ci * TA:(ci + 1) * TA],
                                      proj_ps[gi % 2][0:w, :], AF.Identity,
                                      bias=v_dq[0:w, ci:ci + 1]).then_inc(s_evac, 1)
                    elif ci < NCKN:
                        W(sc, s_norm, NSQ1 * (i - 1) + NCQ + ci + 1)
                        sc.activation(kvf[0:w, ci * TA:(ci + 1) * TA],
                                      proj_ps[gi % 2][0:w, :], AF.Identity,
                                      bias=v_dkv[0:w, ci:ci + 1]).then_inc(s_evac, 1)
                    else:
                        W(sc, st_kr[pa], 16 * (i // 2))
                        sc.activation(kr_st[pa], proj_ps[gi % 2][0:w, :], AF.Identity,
                                      bias=v_dkv[0:w, ci:ci + 1]).then_inc(s_evac, 1)
                    if kind == "q" and ci == NCQ - 1:
                        _a1_rows_act(sc, cs, rows, "q", s_rv, s_ra, 2 * i + 1)
                _a1_rows_act(sc, cs, rows, "k", s_rv, s_ra, 2 * i + 2)
            sc.wait_ge(s_ra, 2 * NTA)
            sc.sem_inc(bar_out, 1)

        @block.vector
        def _(v):
            _bs(v, bar_in, env)
            cv = Chain(v, ch_v1)
            v.wait_ge(env["c_ini"], 1)
            for i in range(NTA):
                pa = i % 2
                for si, (kind, ci, w) in enumerate(sqchunks):
                    sg = NSQ1 * i + si
                    gi = ci if kind == "q" else NCQ + ci
                    v.wait_ge(s_evac, NG1 * i + gi + 1)
                    W(v, s_stat, sg - 1)
                    src = cqf if kind == "q" else kvf
                    v.tensor_mul(sq_scr[sg % 2][0:w, :],
                                 src[0:w, ci * TA:(ci + 1) * TA],
                                 src[0:w, ci * TA:(ci + 1) * TA]).then_inc(s_sq, 1)
                    if kind == "q" and ci == NCQ - 1:
                        _a1_rows_dve(v, cv, rows, "q", Sq_ps, SSq_ps, DQ,
                                     s_stat, NSQ1 * i + NCQ, s_ra, 2 * i + 1,
                                     s_rv, s_rows, s_bc, 2 * i - 1)
                    if kind == "kv" and ci == NCKN - 1:
                        _a1_rows_dve(v, cv, rows, "k", Skv_ps, SSkv_ps, 1024,
                                     s_stat, NSQ1 * i + NSQ1, s_ra, 2 * i + 2,
                                     s_rv, s_rows, s_bc, 2 * i)
                for si, (kind, ci, w) in enumerate(sqchunks):
                    bcidx = 2 * i + (1 if kind == "q" else 2)
                    v.wait_ge(s_bc, bcidx)
                    W(v, (st_cq if kind == "q" else st_kv)[pa], 16 * (i // 2))
                    src = cqf if kind == "q" else kvf
                    dst = cq_st if kind == "q" else kv_st
                    vg = v_qg if kind == "q" else v_kvg
                    vb = v_qb if kind == "q" else v_kvb
                    tp = tmp_n[si % 2]
                    cv(v.tensor_mul(tp[0:w, :], src[0:w, ci * TA:(ci + 1) * TA],
                                    bca_ps[0:w, :]))
                    cv(v.tensor_add(tp[0:w, :], tp[0:w, :], bcb_ps[0:w, :]))
                    v.tensor_scalar(dst[pa][0:w, ci * TA:(ci + 1) * TA], tp[0:w, :],
                                    vg[0:w, ci:ci + 1], vb[0:w, ci:ci + 1],
                                    ALU.mult, ALU.add).then_inc(s_norm, 1)
            v.wait_ge(s_norm, NSQ1 * NTA)
            v.sem_inc(bar_out, 1)


def _a1_spills(env, g, i, s_norm, s_evac, st_cq, st_kr, st_kv, cq_st, kv_st, kr_st):
    pa = i % 2
    W(g, s_norm, NSQ1 * i + NCQ)
    g.dma_start(
        out=env["sc_cq"].rearrange("(c p) t -> p c t", p=128)[:, :, i * TA:(i + 1) * TA],
        in_=cq_st[pa].rearrange("p (c t) -> p c t", c=NCQ),
    ).then_inc(st_cq[pa], 16)
    W(g, s_evac, NG1 * i + NG1)
    g.dma_start(out=env["sc_kr"][:, i * TA:(i + 1) * TA],
                in_=kr_st[pa]).then_inc(st_kr[pa], 16)
    W(g, s_norm, NSQ1 * i + NSQ1)
    g.dma_start(
        out=env["sc_kvn"].rearrange("(c p) t -> p c t", p=128)[:, :, i * TA:(i + 1) * TA],
        in_=kv_st[pa].rearrange("p (c t) -> p c t", c=NCKN),
    ).then_inc(st_kv[pa], 16)


def _a1_stats(t, env, i, gi, s_evac, s_sq, s_stat, s_rows,
              cqf, kvf, sq_scr, Sq_ps, SSq_ps, Skv_ps, SSkv_ps, ones_f):
    if gi < NCQ:
        kind, ci, w, si = "q", gi, QW[gi], gi
        S, SS, src, nch = Sq_ps, SSq_ps, cqf, NCQ
    elif gi < NCQ + NCKN:
        kind, ci, w, si = "kv", gi - NCQ, 128, gi
        S, SS, src, nch = Skv_ps, SSkv_ps, kvf, NCKN
    else:
        return
    first = (ci == 0)
    last = (ci == nch - 1)
    sg = NSQ1 * i + si
    t.wait_ge(s_sq, sg + 1)
    if first:
        W(t, s_rows, 2 * (i - 1) + (1 if kind == "q" else 2))
    t.matmul(S, env["ones_bf"][0:w, :], src[0:w, ci * TA:(ci + 1) * TA],
             start=first, stop=last)
    t.matmul(SS, env["ones_bf"][0:w, :], sq_scr[sg % 2][0:w, :],
             start=first, stop=last).then_inc(s_stat, 1)


def _a1_rows_dve(v, cv, rows, pref, S_ps, SS_ps, dim, s_stat, statv, s_ra, rav,
                 s_rv, s_rows, s_bc, bc_war):
    mu, t1, a, b2 = rows[pref + "mu"], rows[pref + "t1"], rows[pref + "a"], rows[pref + "b2"]
    v.wait_ge(s_stat, statv)
    W(v, s_bc, bc_war)
    inv = 1.0 / float(dim)
    cv(v.tensor_scalar_mul(mu, S_ps, inv))
    cv(v.tensor_scalar_mul(t1, SS_ps, inv))
    cv(v.tensor_mul(a, mu, mu))
    cv(v.tensor_sub(t1, t1, a))
    v.tensor_scalar_add(t1, t1, EPS).then_inc(s_rv, 1)
    v.wait_ge(s_ra, rav)
    cv(v.tensor_mul(b2, mu, a))
    v.tensor_scalar_mul(b2, b2, -1.0).then_inc(s_rows, 1)


def _a1_rows_act(sc, cs, rows, pref, s_rv, s_ra, rav):
    t1, a = rows[pref + "t1"], rows[pref + "a"]
    sc.wait_ge(s_rv, rav)
    cs(sc.activation(a, t1, AF.Ln))
    sc.activation(a, a, AF.Exp, scale=-0.5).then_inc(s_ra, 1)


def emit_A2(env, bar_in, bar_out):
    nc, block, sem = env["nc"], env["block"], env["sem"]
    ident_bf, ident_f = env["ident_bf"], env["ident_f"]
    c_ldg, c_lds = env["c_ldg"], env["c_lds"]

    with ExitStack() as ctx:
        sbl = lambda n, s, dt: ctx.enter_context(nc.sbuf_tensor(n, s, dt)).ap()
        ps = lambda n, s, dt=F32: ctx.enter_context(nc.psum_tensor(n, s, dt)).ap()

        wuq_bf = sbl("a2_wuq", [128, NCQ * D], BF16)
        wukv_bf = sbl("a2_wukv", [128, NCKN * KVU], BF16)
        v_uq = sbl("a2_vuq", [128, NCD], F32)
        v_ukv = sbl("a2_vukv", [128, NCKVU], F32)
        vb_t = sbl("a2_vbt", [16, 128], F32)
        bf16c = sbl("a2_bf16c", [16, 32], F32)
        bf16s = sbl("a2_bf16s", [16, 32], F32)
        bf_t1 = sbl("a2_bft1", [16, 32], F32)
        bf_t2 = sbl("a2_bft2", [16, 32], F32)
        cq_bf = [sbl(f"a2_cq{k}", [128, NCQ * TA], BF16) for k in range(2)]
        kvn_bf = [sbl(f"a2_kvn{k}", [128, NCKN * TA], BF16) for k in range(2)]
        q_st = [sbl(f"a2_qst{k}", [128, NCD * TA], BF16) for k in range(2)]
        kn_st = [sbl(f"a2_knst{k}", [128, NCKN * TA], BF16) for k in range(2)]
        vt_only = sbl("a2_vtst", [128, H * TA], BF16)
        vt_st = [vt_only, vt_only]
        v_ev = [sbl(f"a2_vev{k}", [128, TA], BF16) for k in range(2)]

        proj_ps = [ps(f"a2_pp{k}", [128, TA]) for k in range(3)]
        tr_ps = [ps(f"a2_tr{k}", [128, 128], BF16) for k in range(2)]
        bf_ps = ps("a2_bfps", [128, 128])

        s_ldw = env["psem"]("a2_ldw")
        s_ldf = sem("a2_ldf")
        s_ldq = [sem("a2_ldq0"), sem("a2_ldq1")]
        s_fold = sem("a2_fold")
        s_bftr = sem("a2_bftr")
        s_bfev = sem("a2_bfev")
        s_ldt = [sem("a2_ldt0"), sem("a2_ldt1")]
        s_proj = sem("a2_proj")
        s_evac = sem("a2_evac")
        s_tr = sem("a2_tr")
        s_trev = sem("a2_trev")
        ch_v2 = sem("a2_chv")
        st_q = [env["psem"]("a2_stq0"), env["psem"]("a2_stq1")]
        st_kn = [env["psem"]("a2_stkn0"), env["psem"]("a2_stkn1")]
        st_vt = [env["psem"]("a2_stvt0"), env["psem"]("a2_stvt1")]

        NFOLD = NCQ + 1

        with ExitStack() as fctx:
            fsb = lambda n, s, dt: fctx.enter_context(nc.sbuf_tensor(n, s, dt)).ap()
            wuqf_c = [fsb(f"a2_wuqf{k}", [128, D], F32) for k in range(2)]
            fold_c = fsb("a2_foldc", [128, 512], F32)
            fold_s = fsb("a2_folds", [128, 512], F32)
            fold_t1 = fsb("a2_ft1", [128, 512], F32)
            fold_t2 = fsb("a2_ft2", [128, 512], F32)

            @block.sync
            def _(s):
                _bs(s, bar_in, env)
                s.dma_start(out=fold_c, in_=env["c_foldc"]).then_inc(s_ldf, 16)
                s.dma_start(out=fold_s, in_=env["c_folds"]).then_inc(s_ldf, 16)
                s.dma_start(out=v_uq, in_=env["vuq"]).then_inc(s_ldf, 16)
                s.dma_start(out=v_ukv, in_=env["vukv"]).then_inc(s_ldf, 16)
                s.dma_start(out=bf16c, in_=env["c_bfc"]).then_inc(s_ldf, 16)
                s.dma_start(out=bf16s, in_=env["c_bfs"]).then_inc(s_ldf, 16)
                for ci in range(NCQ):
                    W(s, s_fold, ci - 1)
                    s.dma_start(out=wuqf_c[ci % 2][0:QW[ci], :],
                                in_=env["wuqT"][ci * 128: ci * 128 + QW[ci], :]
                                ).then_inc(s_ldq[ci % 2], 16)

            @block.tensor
            def _(t):
                _bs(t, bar_in, env)
                t.wait_ge(s_ldf, 16 * 6)
                t.wait_ge(c_ldg, 48)
                t.wait_ge(c_lds, 16)
                t.matmul(bf_ps[0:16, :], v_uq, ident_f,
                         start=True, stop=True).then_inc(s_bftr, 1)
                t.wait_ge(s_fold, NFOLD)
                t.matmul(bf_ps[0:128, 0:16], vb_t, ident_f[0:16, 0:16],
                         start=True, stop=True).then_inc(s_bftr, 1)

            @block.scalar
            def _(sc):
                _bs(sc, bar_in, env)
                sc.wait_ge(s_bftr, 1)
                sc.activation(vb_t, bf_ps[0:16, :], AF.Copy).then_inc(s_bfev, 1)
                sc.wait_ge(s_bftr, 2)
                sc.activation(v_uq, bf_ps[0:128, 0:16], AF.Copy).then_inc(s_bfev, 1)

            @block.vector
            def _(v):
                _bs(v, bar_in, env)
                cv = Chain(v, ch_v2)
                v.wait_ge(s_ldf, 16 * 6)
                for ci in range(NCQ):
                    w = QW[ci]
                    v.wait_ge(s_ldq[ci % 2], 16 * (ci // 2 + 1))
                    wf = wuqf_c[ci % 2]
                    r4 = wf.rearrange("p (h g j) -> p h g j", h=H, g=4)
                    Ac = r4[0:w, :, 2, :]
                    Bc = r4[0:w, :, 3, :]
                    fc = fold_c.rearrange("p (h j) -> p h j", h=H)[0:w, :, :]
                    fs = fold_s.rearrange("p (h j) -> p h j", h=H)[0:w, :, :]
                    t1 = fold_t1.rearrange("p (h j) -> p h j", h=H)[0:w, :, :]
                    t2 = fold_t2.rearrange("p (h j) -> p h j", h=H)[0:w, :, :]
                    cv(v.tensor_mul(t1, Bc, fs))
                    cv(v.tensor_mul(t2, Ac, fs))
                    cv(v.tensor_mul(Ac, Ac, fc))
                    cv(v.tensor_add(Ac, Ac, t1))
                    cv(v.tensor_mul(Bc, Bc, fc))
                    cv(v.tensor_sub(Bc, Bc, t2))
                    v.tensor_copy(wuq_bf[0:w, ci * D:(ci + 1) * D],
                                  wf[0:w, :]).then_inc(s_fold, 1)
                v.wait_ge(s_bfev, 1)
                A = vb_t[:, 64:96]
                B = vb_t[:, 96:128]
                cv(v.tensor_mul(bf_t1, B, bf16s))
                cv(v.tensor_mul(bf_t2, A, bf16s))
                cv(v.tensor_mul(A, A, bf16c))
                cv(v.tensor_add(A, A, bf_t1))
                cv(v.tensor_mul(B, B, bf16c))
                v.tensor_sub(B, B, bf_t2).then_inc(s_fold, 1)
                v.wait_ge(s_bfev, 2)
                v.sem_inc(bar_out, 1)

        @block.gpsimd
        def _(g):
            _bs(g, bar_in, env)
            g.dma_start(out=wukv_bf.rearrange("p (c j) -> p c j", c=NCKN),
                        in_=env["wukvT"].rearrange("(c p) j -> p c j", p=128)).then_inc(s_ldw, 16)
            for i in range(NTA):
                pa = i % 2
                W(g, s_evac, NG2 * i + NGQ2)
                g.dma_start(
                    out=env["sc_q"].rearrange("(c p) t -> p c t", p=128)[:, :, i * TA:(i + 1) * TA],
                    in_=q_st[pa].rearrange("p (c t) -> p c t", c=NCD),
                ).then_inc(st_q[pa], 16)
                W(g, s_evac, NG2 * i + NGQ2 + NCKN)
                g.dma_start(
                    out=env["sc_kn"].rearrange("(c p) t -> p c t", p=128)[:, :, i * TA:(i + 1) * TA],
                    in_=kn_st[pa].rearrange("p (c t) -> p c t", c=NCKN),
                ).then_inc(st_kn[pa], 16)
                W(g, s_trev, 4 * H * (i + 1))
                for sub in range(NSUB):
                    g.dma_start(
                        out=env["sc_vt"].rearrange("(h tt sub p) d -> tt sub p h d",
                                                   h=H, tt=NTA, sub=NSUB)[i, sub],
                        in_=vt_st[pa].rearrange("p (h sub d) -> p h sub d",
                                                h=H, sub=NSUB)[:, :, sub, :],
                    ).then_inc(st_vt[pa], 16)
            for sm in (*st_q, *st_kn):
                g.wait_ge(sm, 16 * (NTA // 2))
            for sm in st_vt:
                g.wait_ge(sm, 16 * NSUB * (NTA // 2))
            g.sem_inc(bar_out, 1)

        @block.sync
        def _(s):
            for i in range(NTA):
                pa = i % 2
                W(s, s_proj, NG2 * (i - 1))
                s.dma_start(
                    out=cq_bf[pa].rearrange("p (c t) -> p c t", c=NCQ),
                    in_=env["sc_cq"].rearrange("(c p) t -> p c t", p=128)[:, :, i * TA:(i + 1) * TA],
                ).then_inc(s_ldt[pa], 16)
                s.dma_start(
                    out=kvn_bf[pa].rearrange("p (c t) -> p c t", c=NCKN),
                    in_=env["sc_kvn"].rearrange("(c p) t -> p c t", p=128)[:, :, i * TA:(i + 1) * TA],
                ).then_inc(s_ldt[pa], 16)
            s.sem_inc(bar_out, 1)

        @block.tensor
        def _(t):
            t.wait_ge(s_fold, NCQ)
            t.wait_ge(s_bfev, 2)
            t.wait_ge(s_ldw, 16)
            for i in range(NTA):
                pa = i % 2
                t.wait_ge(s_ldt[pa], 32 * (i // 2 + 1))
                for j in range(NGQ2):
                    gg = NG2 * i + j
                    W(t, s_evac, gg - 2)
                    for ci in range(NCQ):
                        w = QW[ci]
                        mi = t.matmul(proj_ps[gg % 3],
                                      wuq_bf[0:w, ci * D + j * 128: ci * D + (j + 1) * 128],
                                      cq_bf[pa][0:w, ci * TA:(ci + 1) * TA],
                                      start=(ci == 0), stop=(ci == NCQ - 1))
                    mi.then_inc(s_proj, 1)
                def emit_trs(t, h):
                    t.wait_ge(s_evac, NG2 * i + NGQ2 + NCKN + h + 1)
                    for sub in range(NSUB):
                        gs = NSUB * (H * i + h) + sub
                        W(t, s_trev, gs - 1)
                        t.transpose(tr_ps[gs % 2],
                                    v_ev[h % 2][:, sub * 128:(sub + 1) * 128],
                                    ident_bf).then_inc(s_tr, 1)

                for j in range(NCKVU):
                    gg = NG2 * i + NGQ2 + j
                    W(t, s_evac, gg - 2)
                    for ci in range(NCKN):
                        mi = t.matmul(proj_ps[gg % 3],
                                      wukv_bf[:, ci * KVU + j * 128: ci * KVU + (j + 1) * 128],
                                      kvn_bf[pa][:, ci * TA:(ci + 1) * TA],
                                      start=(ci == 0), stop=(ci == NCKN - 1))
                    mi.then_inc(s_proj, 1)
                    if j >= NCKN + 1:
                        emit_trs(t, j - NCKN - 1)
                emit_trs(t, H - 1)
            t.wait_ge(s_tr, NSUB * H * NTA)
            t.sem_inc(bar_out, 1)

        @block.scalar
        def _(sc):
            for i in range(NTA):
                pa = i % 2
                for j in range(NGQ2):
                    gg = NG2 * i + j
                    sc.wait_ge(s_proj, gg + 1)
                    W(sc, st_q[pa], 16 * (i // 2))
                    sc.activation(q_st[pa][:, j * TA:(j + 1) * TA], proj_ps[gg % 3],
                                  AF.Identity, bias=v_uq[:, j:j + 1]).then_inc(s_evac, 1)
                for j in range(NCKVU):
                    gg = NG2 * i + NGQ2 + j
                    sc.wait_ge(s_proj, gg + 1)
                    if j < NCKN:
                        W(sc, st_kn[pa], 16 * (i // 2))
                        sc.activation(kn_st[pa][:, j * TA:(j + 1) * TA], proj_ps[gg % 3],
                                      AF.Identity, bias=v_ukv[:, j:j + 1]).then_inc(s_evac, 1)
                    else:
                        h = j - NCKN
                        W(sc, s_tr, NSUB * (H * i + h - 1))
                        sc.activation(v_ev[h % 2], proj_ps[gg % 3],
                                      AF.Identity, bias=v_ukv[:, j:j + 1]).then_inc(s_evac, 1)
                        for sub in range(NSUB):
                            gs = NSUB * (H * i + h) + sub
                            sc.wait_ge(s_tr, gs + 1)
                            if i > 0:
                                W(sc, st_vt[(i - 1) % 2], 16 * NSUB * ((i - 1) // 2 + 1))
                            sc.activation(vt_st[pa][:, h * TA + sub * 128: h * TA + (sub + 1) * 128],
                                          tr_ps[gs % 2], AF.Copy).then_inc(s_trev, 1)
            sc.wait_ge(s_trev, NSUB * H * NTA)
            sc.sem_inc(bar_out, 1)


def emit_B(env, bar_in, bar_out):
    nc, block, sem = env["nc"], env["block"], env["sem"]
    ones_bf, onesr_f, masks_bf = env["ones_bf"], env["onesr_f"], env["masks_bf"]
    ones_f = env["ones_f"]

    visits = []
    for h in range(BH):
        for i in range(NTB):
            nck = 4 * (i + 1)
            for ck in range(nck):
                visits.append((h, i, ck, ck >= 4 * i, ck == nck - 1))
    NV = len(visits)
    VH = 144
    dcum = []
    dc = 0
    for (_, _, _, dg, _) in visits:
        dc += 1 if dg else 0
        dcum.append(dc)

    with ExitStack() as ctx:
        sbl = lambda n, s, dt: ctx.enter_context(nc.sbuf_tensor(n, s, dt)).ap()
        ps = lambda n, s, dt=F32: ctx.enter_context(nc.psum_tensor(n, s, dt)).ap()

        q_t = [sbl(f"b_q{k}", [128, T], BF16) for k in range(2)]
        k_t = [sbl(f"b_k{k}", [128, T], BF16) for k in range(2)]
        v_t = [sbl(f"b_v{k}", [128, T], BF16) for k in range(2)]
        p_sb = [sbl(f"b_p{k}", [128, TB], BF16) for k in range(3)]
        sq_scr = sbl("b_sq", [128, 16 * TB], BF16)
        redq8 = sbl("b_rq8", [8, 1], F32)
        redk8 = sbl("b_rk8", [8, 1], F32)
        l_t = sbl("b_lt", [1, TB], F32)
        nrm8q = sbl("b_n8q", [1, 8], F32)
        nrm8k = sbl("b_n8k", [1, 8], F32)
        nsc = sbl("b_nsc", [1, 8], F32)
        negB_row = sbl("b_negBr", [1, 1], F32)
        negB_sb = sbl("b_negB", [128, 1], F32)
        l_sb = sbl("b_l", [1, TB], F32)
        r_sb = sbl("b_r", [1, TB], F32)
        rbc_sb = sbl("b_rbc", [128, TB], F32)
        y_stage = [sbl(f"b_yst{k}", [128, T], BF16) for k in range(2)]
        conv_sb = [sbl(f"b_cv{k}", [128, D], BF16) for k in range(2)]

        s_ps = [ps(f"b_s{k}", [128, TB]) for k in range(3)]
        y_ps = [ps(f"b_y{k}", [128, TB]) for k in range(2)]
        l_ps = [ps(f"b_lp{k}", [8, TB]) for k in range(2)]
        rbc_ps = ps("b_rbcp", [128, TB])
        nb_ps = rbc_ps[:, 0:1]

        s_ldb = [sem("b_ldb0"), sem("b_ldb1")]
        s_s = sem("b_s")
        s_p = sem("b_p")
        s_pm = sem("b_pm")
        s_ly = sem("b_ly")
        s_lv = sem("b_lv")
        s_fr = sem("b_fr")
        s_bc = sem("b_bc")
        s_rbv = sem("b_rbv")
        s_yf = sem("b_yf")
        s_sq = sem("b_sq")
        s_nm = sem("b_nm")
        s_nr = sem("b_nr")
        s_np = sem("b_np")
        s_nle = sem("b_nle")
        s_nrow = sem("b_nrow")
        s_nbm = sem("b_nbm")
        s_nbe = sem("b_nbe")
        s_ns = sem("b_ns")
        ch_vb = sem("b_chv")
        ch_sb = sem("b_chs")
        s_sty = [env["psem"]("b_sty0"), env["psem"]("b_sty1")]
        s_ldc = [env["psem"]("b_ldc0"), env["psem"]("b_ldc1")]
        s_stc = [env["psem"]("b_stc0"), env["psem"]("b_stc1")]
        s_cva = sem("b_cva")

        @block.sync
        def _(s):
            _bs(s, bar_in, env)
            for h in range(BH):
                pa = h % 2
                W(s, s_ly, VH * (h - 1))
                s.dma_start(out=q_t[pa],
                            in_=env["sc_q"].rearrange("(c p) t -> p c t", p=128)[:, h, :]
                            ).then_inc(s_ldb[pa], 16)
                s.dma_start(out=k_t[pa][0:64, :],
                            in_=env["sc_kn"][h * 64:(h + 1) * 64, :]).then_inc(s_ldb[pa], 16)
                s.dma_start(out=k_t[pa][64:128, :],
                            in_=env["sc_kr"]).then_inc(s_ldb[pa], 16)
                s.dma_start(out=v_t[pa].rearrange("p (ck d) -> p ck d", ck=T // 128),
                            in_=env["sc_vt"].rearrange("(hh ck p) d -> hh p ck d",
                                                       hh=H, p=128)[h]).then_inc(s_ldb[pa], 16)
            s.sem_inc(bar_out, 1)

        @block.gpsimd
        def _(g):
            _bs(g, bar_in, env)
            if "wo_bf" in env:
                env["wo_prefetched"] = True
                g.dma_start(out=env["wo_bf"].rearrange("p (c j) -> p c j", c=NCD),
                            in_=env["woT"].rearrange("(c p) j -> p c j", p=128)
                            ).then_inc(env["c_ldw"], 16)
            cvj = [0]

            def conv_step(g):
                j = cvj[0]
                if j >= NCH or not BCONV:
                    return
                cvj[0] += 1
                W(g, s_stc[j % 2], 16 * (j // 2))
                g.dma_start(out=conv_sb[j % 2],
                            in_=env["wdT"][j * 128:(j + 1) * 128, :]
                            ).then_inc(s_ldc[j % 2], 16)
                g.wait_ge(s_cva, j + 1)
                g.dma_start(
                    out=env["sc_wd"].rearrange("(cb p) (hc m) -> p cb hc m",
                                               p=128, m=128)[:, :, j, :],
                    in_=conv_sb[j % 2].rearrange("p (cb m) -> p cb m", m=128),
                ).then_inc(s_stc[j % 2], 16)

            for h in range(BH):
                for _ in range(3):
                    conv_step(g)
                if BFLASH:
                    g.wait_ge(s_yf, NTB * (h + 1))
                    g.dma_start(out=env["sc_y"].rearrange("(c p) t -> p c t", p=128)[:, h, :],
                                in_=y_stage[h % 2]).then_inc(s_sty[h % 2], 16)
            while BCONV and cvj[0] < NCH:
                conv_step(g)
            if BCONV:
                g.wait_ge(s_stc[0], 16 * ((NCH + 1) // 2))
                g.wait_ge(s_stc[1], 16 * (NCH // 2))
            if BFLASH:
                W(g, s_sty[0], 16 * ((BH + 1) // 2))
                W(g, s_sty[1], 16 * (BH // 2))
            if env.get("dbg_b"):
                dbg = env["sc_dbg"]
                g.dma_start(out=dbg[0:1, 0:8], in_=nsc).then_inc(s_ldc[0], 16)
                with nc.allow_non_contiguous_dma(reason="debug dump"):
                    g.dma_start(out=dbg[:, 8:9], in_=negB_sb).then_inc(s_ldc[0], 16)
                g.dma_start(out=dbg[0:1, 16:16 + TB], in_=l_sb).then_inc(s_ldc[0], 16)
                g.dma_start(out=dbg[0:1, 16 + TB:16 + 2 * TB],
                            in_=r_sb).then_inc(s_ldc[0], 16)
                g.dma_start(out=dbg[0:8, 16 + 2 * TB:16 + 3 * TB],
                            in_=redq8).then_inc(s_ldc[0], 16) if False else None
                g.wait_ge(s_ldc[0], 16 * ((NCH + 1) // 2) * 0 + 16 * 4 + (16 * ((NCH + 1) // 2) if BCONV else 0))
            g.sem_inc(bar_out, 1)

        @block.tensor
        def _(t):
            _bs(t, bar_in, env)
            pend_bc = [None]

            def flush_bc(t):
                if pend_bc[0] is None:
                    return
                f = pend_bc[0]
                pend_bc[0] = None
                t.wait_ge(s_fr, f + 1)
                W(t, s_rbv, f)
                t.matmul(rbc_ps, onesr_f, r_sb, start=True, stop=True).then_inc(s_bc, 1)

            t.wait_ge(env["c_ini"], 1)
            for h in range(BH):
                pa = h % 2
                t.wait_ge(s_ldb[pa], 64 * (h // 2 + 1))
                W(t, s_lv, 8 * h)

                def emit_ly(t, vv):
                    _, i, ck, diag, last = visits[vv]
                    f = 8 * h + i
                    if diag:
                        t.wait_ge(s_pm, dcum[vv])
                    else:
                        t.wait_ge(s_p, vv + 1)
                    if ck == 0:
                        W(t, s_yf, f - 1)
                        W(t, s_lv, f - 1)
                    t.matmul(l_ps[i % 2][0:1, :], ones_bf, p_sb[vv % 3],
                             start=(ck == 0), stop=last)
                    t.matmul(y_ps[i % 2], v_t[pa][:, ck * 128:(ck + 1) * 128],
                             p_sb[vv % 3], start=(ck == 0), stop=last).then_inc(s_ly, 1)
                    if last:
                        pend_bc[0] = f

                for vv in (range(VH * h, VH * (h + 1)) if BFLASH else ()):
                    _, i, ck, diag, last = visits[vv]
                    if ck == 6:
                        flush_bc(t)
                    if vv < 3:
                        W(t, s_p, vv - 2)
                    t.matmul(s_ps[vv % 3], k_t[pa][:, ck * 128:(ck + 1) * 128],
                             q_t[pa][:, i * TB:(i + 1) * TB],
                             start=True, stop=True).then_inc(s_s, 1)
                    if vv - 2 >= VH * h:
                        emit_ly(t, vv - 2)
                if BFLASH:
                    emit_ly(t, VH * (h + 1) - 2)
                    emit_ly(t, VH * (h + 1) - 1)
                flush_bc(t)
            W(t, s_bc, 8 * BH if BFLASH else 0)
            t.sem_inc(bar_out, 1)

        @block.scalar
        def _(sc):
            _bs(sc, bar_in, env)
            csb = Chain(sc, ch_sb)
            for h in range(BH):

                def emit_r(sc, vv2):
                    f2 = 8 * h + visits[vv2][1]
                    sc.wait_ge(s_lv, f2 + 1)
                    csb(sc.activation(l_t, l_sb, AF.Ln))
                    W(sc, s_bc, f2)
                    sc.activation(r_sb, l_t, AF.Exp, scale=-1.0).then_inc(s_fr, 1)

                if h == 0:
                    sc.wait_ge(s_nrow, 1)
                for vv in (range(VH * h, VH * (h + 1)) if BFLASH else ()):
                    _, i, ck, diag, last = visits[vv]
                    sc.wait_ge(s_s, vv + 1)
                    W(sc, s_ly, vv - 2)
                    sc.activation(p_sb[vv % 3], s_ps[vv % 3], AF.Exp,
                                  bias=negB_sb, scale=SCALE).then_inc(s_p, 1)
                    if vv - 2 >= VH * h and visits[vv - 2][4]:
                        emit_r(sc, vv - 2)
                if BFLASH:
                    if visits[VH * (h + 1) - 2][4]:
                        emit_r(sc, VH * (h + 1) - 2)
                    emit_r(sc, VH * (h + 1) - 1)
            W(sc, s_fr, 8 * BH if BFLASH else 0)
            sc.sem_inc(bar_out, 1)

        @block.vector
        def _(v):
            _bs(v, bar_in, env)
            cvb = Chain(v, ch_vb)
            W(v, env["c_ldg"], 48)

            cvj2 = [0]

            def conv_ts(v):
                j = cvj2[0]
                if j >= NCH or not BCONV:
                    return
                cvj2[0] += 1
                W(v, s_stc[j % 2], 16 * (j // 2))
                v.wait_ge(s_ldc[j % 2], 16 * (j // 2 + 1))
                v.tensor_scalar_add(conv_sb[j % 2], conv_sb[j % 2],
                                    -128.0).then_inc(s_cva, 1)

            v.memset(negB_sb, -20.0).then_inc(s_nrow, 1)
            for h in range(BH):
                pa = h % 2
                for vv in (range(VH * h, VH * (h + 1)) if BFLASH else ()):
                    _, i, ck, diag, last = visits[vv]
                    f = 8 * h + i
                    if vv % 48 == 24:
                        conv_ts(v)
                    if diag:
                        v.wait_ge(s_p, vv + 1)
                        off = ck - 4 * i
                        v.tensor_mul(p_sb[vv % 3], p_sb[vv % 3],
                                     masks_bf[:, off * TB:(off + 1) * TB]
                                     ).then_inc(s_pm, 1)
                    if last:
                        v.wait_ge(s_ly, vv + 1)
                        v.tensor_copy(l_sb, l_ps[i % 2][0:1, :]).then_inc(s_lv, 1)
                        v.wait_ge(s_bc, f + 1)
                        v.tensor_copy(rbc_sb, rbc_ps).then_inc(s_rbv, 1)
                        v.wait_ge(s_rbv, f + 1)
                        if i == 0:
                            W(v, s_sty[h % 2], 16 * (h // 2))
                        v.tensor_mul(y_stage[h % 2][:, i * TB:(i + 1) * TB],
                                     y_ps[i % 2], rbc_sb).then_inc(s_yf, 1)
            while BCONV and cvj2[0] < NCH:
                conv_ts(v)
            W(v, s_yf, 8 * BH if BFLASH else 0)
            W(v, s_cva, NCH if BCONV else 0)
            v.sem_inc(bar_out, 1)


def emit_C(env, bar_in, bar_out):
    nc, block, sem = env["nc"], env["block"], env["sem"]
    ones_f, onesr_f = env["ones_f"], env["onesr_f"]

    with ExitStack() as ctx:
        sbl = lambda n, s, dt: ctx.enter_context(nc.sbuf_tensor(n, s, dt)).ap()
        ps = lambda n, s: ctx.enter_context(nc.psum_tensor(n, s, F32)).ap()

        wo_bf = env["wo_bf"]
        v_o = sbl("c_vo", [128, NCD], F32)
        y_t = [sbl(f"c_y{k}", [128, NCD * TC], BF16) for k in range(2)]
        xu_t = [sbl(f"c_xu{k}", [128, NCD * TC], F32) for k in range(2)]
        z_sb = [sbl(f"c_z{k}", [128, TC], F32) for k in range(2)]
        sq_scr = [sbl(f"c_sq{k}", [128, TC], BF16) for k in range(2)]
        ub_scr = [sbl(f"c_ub{k}", [128, TC], BF16) for k in range(2)]
        rows = {}
        for rn in ("mu", "t1", "a", "b2"):
            rows["q" + rn] = sbl(f"c_r_{rn}", [1, TC], F32)

        z_ps = [ps(f"c_zp{k}", [128, TC]) for k in range(3)]
        S_ps = ps("c_S", [1, TC])
        SS_ps = ps("c_SS", [1, TC])
        bca_ps = ps("c_bca", [128, TC])
        bcb_ps = ps("c_bcb", [128, TC])
        rows["_S"] = S_ps
        rows["_SS"] = SS_ps

        c_ldw = env["c_ldw"]
        c_ldv = sem("c_ldv")
        c_ldt = [sem("c_ldt0"), sem("c_ldt1")]
        c_mm = sem("c_mm")
        c_ev = sem("c_ev")
        c_add = sem("c_add")
        c_sq = sem("c_sq")
        c_stat = sem("c_stat")
        c_rv = sem("c_rv")
        c_ra = sem("c_ra")
        c_rows = sem("c_rows")
        c_bc = sem("c_bc")
        c_norm = sem("c_norm")
        st_x1 = [env["psem"]("c_stx0"), env["psem"]("c_stx1")]
        ch_vc = sem("c_chv")
        ch_sc = sem("c_chs")

        @block.gpsimd
        def _(g):
            _bs(g, bar_in, env)
            if not env.get("wo_prefetched"):
                g.dma_start(out=wo_bf.rearrange("p (c j) -> p c j", c=NCD),
                            in_=env["woT"].rearrange("(c p) j -> p c j", p=128)
                            ).then_inc(c_ldw, 16)
            for i in range(NTC):
                pa = i % 2
                W(g, c_norm, NCD * (i + 1))
                g.dma_start(
                    out=env["sc_x1f"].rearrange("(c p) t -> p c t", p=128)[:, :, i * TC:(i + 1) * TC],
                    in_=xu_t[pa].rearrange("p (c t) -> p c t", c=NCD),
                ).then_inc(st_x1[pa], 16)
            for sm in st_x1:
                g.wait_ge(sm, 16 * (NTC // 2))
            g.sem_inc(bar_out, 1)

        @block.sync
        def _(s):
            _bs(s, bar_in, env)
            s.dma_start(out=v_o, in_=env["vo"]).then_inc(c_ldv, 16)
            for i in range(NTC):
                pa = i % 2
                W(s, c_mm, NCD * (i - 1))
                W(s, st_x1[pa], 16 * (i // 2))
                s.dma_start(
                    out=y_t[pa].rearrange("p (c t) -> p c t", c=NCD),
                    in_=env["sc_y"].rearrange("(c p) t -> p c t", p=128)[:, :, i * TC:(i + 1) * TC],
                ).then_inc(c_ldt[pa], 16)
                s.dma_start(
                    out=xu_t[pa].rearrange("p (c t) -> p c t", c=NCD),
                    in_=env["xT"].rearrange("(c p) t -> p c t", p=128)[:, :, i * TC:(i + 1) * TC],
                ).then_inc(c_ldt[pa], 16)
            s.sem_inc(bar_out, 1)

        @block.tensor
        def _(t):
            _bs(t, bar_in, env)
            t.wait_ge(c_ldw, 16)
            t.wait_ge(env["c_ini"], 1)
            pend = [None]

            def flush_cbc(t):
                if pend[0] is None:
                    return
                ii = pend[0]
                pend[0] = None
                t.wait_ge(c_rows, ii + 1)
                W(t, c_norm, NCD * ii)
                t.matmul(bca_ps, onesr_f, rows["qa"], start=True, stop=True)
                t.matmul(bcb_ps, onesr_f, rows["qb2"],
                         start=True, stop=True).then_inc(c_bc, 1)

            for i in range(NTC):
                pa = i % 2
                t.wait_ge(c_ldt[pa], 32 * (i // 2 + 1))
                for c in range(NCD):
                    gg = NCD * i + c
                    W(t, c_ev, gg - 2)
                    for cd in range(NCD):
                        mi = t.matmul(z_ps[gg % 3],
                                      wo_bf[:, cd * D + c * 128: cd * D + (c + 1) * 128],
                                      y_t[pa][:, cd * TC:(cd + 1) * TC],
                                      start=(cd == 0), stop=(cd == NCD - 1))
                    mi.then_inc(c_mm, 1)
                    if c >= 1:
                        if c == 1:
                            flush_cbc(t)
                        _c_stats(t, env, i, c - 1, c_sq, c_stat, c_rows,
                                 ub_scr, sq_scr, S_ps, SS_ps, ones_f)
                _c_stats(t, env, i, NCD - 1, c_sq, c_stat, c_rows,
                         ub_scr, sq_scr, S_ps, SS_ps, ones_f)
                pend[0] = i
            flush_cbc(t)
            t.wait_ge(c_bc, NTC)
            t.sem_inc(bar_out, 1)

        @block.scalar
        def _(sc):
            _bs(sc, bar_in, env)
            cs = Chain(sc, ch_sc)
            sc.wait_ge(c_ldv, 16)
            for i in range(NTC):
                for c in range(NCD):
                    gg = NCD * i + c
                    sc.wait_ge(c_mm, gg + 1)
                    W(sc, c_add, gg - 1)
                    sc.activation(z_sb[gg % 2], z_ps[gg % 3], AF.Identity,
                                  bias=v_o[:, c:c + 1]).then_inc(c_ev, 1)
                _c_rows_act(sc, cs, rows, c_rv, c_ra, i + 1)
            sc.wait_ge(c_ra, NTC)
            sc.sem_inc(bar_out, 1)

        @block.vector
        def _(v):
            _bs(v, bar_in, env)
            cv = Chain(v, ch_vc)
            for i in range(NTC):
                pa = i % 2
                for c in range(NCD):
                    gg = NCD * i + c
                    v.wait_ge(c_ev, gg + 1)
                    v.wait_ge(c_ldt[pa], 32 * (i // 2 + 1))
                    xc = xu_t[pa][:, c * TC:(c + 1) * TC]
                    cv(v.tensor_add(xc, xc, z_sb[gg % 2]))
                    v.engine_nop().then_inc(c_add, 1)
                    W(v, c_stat, gg - 1)
                    v.tensor_copy(ub_scr[gg % 2], xc)
                    v.tensor_mul(sq_scr[gg % 2], xc, xc).then_inc(c_sq, 1)
                    if c == NCD - 1:
                        _c_rows_dve(v, cv, rows, 2048, c_stat, NCD * (i + 1),
                                    c_ra, i + 1, c_rv, c_rows, c_bc, i)
                for c in range(NCD):
                    v.wait_ge(c_bc, i + 1)
                    W(v, st_x1[pa], 16 * (i // 2))
                    xc = xu_t[pa][:, c * TC:(c + 1) * TC]
                    cv(v.tensor_mul(xc, xc, bca_ps))
                    v.tensor_add(xc, xc, bcb_ps).then_inc(c_norm, 1)
            v.wait_ge(c_norm, NCD * NTC)
            v.sem_inc(bar_out, 1)


def _c_stats(t, env, i, c, c_sq, c_stat, c_rows, ub, sq_scr, S_ps, SS_ps, ones_f):
    gg = NCD * i + c
    t.wait_ge(c_sq, gg + 1)
    if c == 0:
        W(t, c_rows, i)
    t.matmul(S_ps, env["ones_bf"], ub[gg % 2],
             start=(c == 0), stop=(c == NCD - 1))
    t.matmul(SS_ps, env["ones_bf"], sq_scr[gg % 2],
             start=(c == 0), stop=(c == NCD - 1)).then_inc(c_stat, 1)


def _c_rows_dve(v, cv, rows, dim, c_stat, statv, c_ra, rav, c_rv, c_rows,
                c_bc, bc_war):
    mu, t1, a, b2 = rows["qmu"], rows["qt1"], rows["qa"], rows["qb2"]
    v.wait_ge(c_stat, statv)
    W(v, c_bc, bc_war)
    S_ps, SS_ps = rows["_S"], rows["_SS"]
    inv = 1.0 / float(dim)
    cv(v.tensor_scalar_mul(mu, S_ps, inv))
    cv(v.tensor_scalar_mul(t1, SS_ps, inv))
    cv(v.tensor_mul(a, mu, mu))
    cv(v.tensor_sub(t1, t1, a))
    v.tensor_scalar_add(t1, t1, EPS).then_inc(c_rv, 1)
    v.wait_ge(c_ra, rav)
    cv(v.tensor_mul(b2, mu, a))
    v.tensor_scalar_mul(b2, b2, -1.0).then_inc(c_rows, 1)


def _c_rows_act(sc, cs, rows, c_rv, c_ra, rav):
    t1, a = rows["qt1"], rows["qa"]
    sc.wait_ge(c_rv, rav)
    cs(sc.activation(a, t1, AF.Ln))
    sc.activation(a, a, AF.Exp, scale=-0.5).then_inc(c_ra, 1)


def emit_G(env, bar_in, bar_out):
    nc, block, sem = env["nc"], env["block"], env["sem"]

    with ExitStack() as ctx:
        sbl = lambda n, s, dt: ctx.enter_context(nc.sbuf_tensor(n, s, dt)).ap()
        ps = lambda n, s: ctx.enter_context(nc.psum_tensor(n, s, F32)).ap()

        x1_bf = sbl("g_x1", [128, NCD * T], BF16)
        wg_sb = [sbl(f"g_wg{k}", [128, NCD * 128], BF16) for k in range(2)]
        wu_sb = [sbl(f"g_wu{k}", [128, NCD * 128], BF16) for k in range(2)]
        v_sg = sbl("g_vsg", [128, NCH], F32)
        v_su = sbl("g_vsu", [128, NCH], F32)
        gate_sb = [sbl(f"g_gt{k}", [128, TC], BF16) for k in range(2)]
        up_sb = [sbl(f"g_up{k}", [128, TC], BF16) for k in range(2)]
        z_st = [sbl(f"g_zst{k}", [128, NTC * TC], BF16) for k in range(2)]

        g_ps = [ps(f"g_g{k}", [128, TC]) for k in range(2)]
        u_ps = [ps(f"g_u{k}", [128, TC]) for k in range(2)]

        g_ldx = env["psem"]("g_ldx")
        g_ldv = sem("g_ldv")
        g_ldw = [env["psem"]("g_ldw0"), env["psem"]("g_ldw1")]
        g_mm = sem("g_mm")
        g_ev = sem("g_ev")
        g_z = sem("g_z")
        st_z = [env["psem"]("g_stz0"), env["psem"]("g_stz1")]

        NK = NTC

        @block.gpsimd
        def _(g):
            _bs(g, bar_in, env)
            g.dma_start(out=x1_bf.rearrange("p (c t) -> p c t", c=NCD),
                        in_=env["sc_x1f"].rearrange("(c p) t -> p c t", p=128)
                        ).then_inc(g_ldx, 16)
            for h in range(NCH):
                pa = h % 2
                W(g, g_mm, NK * (h - 1))
                g.dma_start(out=wg_sb[pa].rearrange("p (c j) -> p c j", c=NCD),
                            in_=env["wgT"].rearrange("(c p) j -> p c j", p=128)[
                                :, :, h * 128:(h + 1) * 128]).then_inc(g_ldw[pa], 16)
                g.dma_start(out=wu_sb[pa].rearrange("p (c j) -> p c j", c=NCD),
                            in_=env["wuT2"].rearrange("(c p) j -> p c j", p=128)[
                                :, :, h * 128:(h + 1) * 128]).then_inc(g_ldw[pa], 16)
                if h >= 1:
                    _g_spill(env, g, h - 1, g_z, st_z, z_st)
            _g_spill(env, g, NCH - 1, g_z, st_z, z_st)
            for sm in st_z:
                g.wait_ge(sm, 16 * (NCH // 2))
            g.sem_inc(bar_out, 1)

        @block.sync
        def _(s):
            _bs(s, bar_in, env)
            s.dma_start(out=v_sg, in_=env["vsg"]).then_inc(g_ldv, 16)
            s.dma_start(out=v_su, in_=env["vsu"]).then_inc(g_ldv, 16)
            s.sem_inc(bar_out, 1)

        @block.tensor
        def _(t):
            _bs(t, bar_in, env)
            t.wait_ge(g_ldx, 16)
            for h in range(NCH):
                pa = h % 2
                t.wait_ge(g_ldw[pa], 32 * (h // 2 + 1))
                for i in range(NK):
                    k = NK * h + i
                    W(t, g_ev, k - 1)
                    for cd in range(NCD):
                        t.matmul(g_ps[k % 2],
                                 wg_sb[pa][:, cd * 128:(cd + 1) * 128],
                                 x1_bf[:, cd * T + i * TC: cd * T + (i + 1) * TC],
                                 start=(cd == 0), stop=(cd == NCD - 1))
                    for cd in range(NCD):
                        mi = t.matmul(u_ps[k % 2],
                                      wu_sb[pa][:, cd * 128:(cd + 1) * 128],
                                      x1_bf[:, cd * T + i * TC: cd * T + (i + 1) * TC],
                                      start=(cd == 0), stop=(cd == NCD - 1))
                    mi.then_inc(g_mm, 1)
            t.wait_ge(g_mm, NK * NCH)
            t.sem_inc(bar_out, 1)

        @block.scalar
        def _(sc):
            _bs(sc, bar_in, env)
            sc.wait_ge(g_ldv, 32)
            for h in range(NCH):
                for i in range(NK):
                    k = NK * h + i
                    sc.wait_ge(g_mm, k + 1)
                    W(sc, g_z, k - 1)
                    sc.activation(gate_sb[k % 2], g_ps[k % 2], AF.Silu,
                                  scale=v_sg[:, h:h + 1])
                    sc.activation(up_sb[k % 2], u_ps[k % 2], AF.Identity,
                                  scale=v_su[:, h:h + 1]).then_inc(g_ev, 1)
            sc.wait_ge(g_ev, NK * NCH)
            sc.sem_inc(bar_out, 1)

        @block.vector
        def _(v):
            _bs(v, bar_in, env)
            for h in range(NCH):
                pa = h % 2
                for i in range(NK):
                    k = NK * h + i
                    v.wait_ge(g_ev, k + 1)
                    if i == 0:
                        W(v, st_z[pa], 16 * (h // 2))
                    v.tensor_mul(z_st[pa][:, i * TC:(i + 1) * TC],
                                 gate_sb[k % 2], up_sb[k % 2]).then_inc(g_z, 1)
            v.wait_ge(g_z, NK * NCH)
            v.sem_inc(bar_out, 1)


def _g_spill(env, g, h, g_z, st_z, z_st):
    pa = h % 2
    W(g, g_z, NTC * (h + 1))
    g.dma_start(out=env["sc_z"][h * 128:(h + 1) * 128, :],
                in_=z_st[pa]).then_inc(st_z[pa], 16)


def emit_D(env, bar_in, bar_out):
    nc, block, sem = env["nc"], env["block"], env["sem"]
    ones_f, onesr_f = env["ones_f"], env["onesr_f"]
    TDL = TC
    NTDL = T // TDL

    with ExitStack() as ctx:
        sbl = lambda n, s, dt: ctx.enter_context(nc.sbuf_tensor(n, s, dt)).ap()
        ps = lambda n, s: ctx.enter_context(nc.psum_tensor(n, s, F32)).ap()

        z_sb = [sbl(f"d_z{k}", [128, NCH * TDL], BF16) for k in range(2)]
        wd_c = [sbl(f"d_wd{k}", [128, NCH * 128], BF16) for k in range(2)]
        xu_t = [sbl(f"d_xu{k}", [128, NCD * TDL], F32) for k in range(2)]
        d_sb = [sbl(f"d_d{k}", [128, TDL], F32) for k in range(2)]
        sq_scr = [sbl(f"d_sq{k}", [128, TDL], BF16) for k in range(2)]
        ub_scr = [sbl(f"d_ub{k}", [128, TDL], BF16) for k in range(2)]
        v_sd = sbl("d_vsd", [128, NCD], F32)
        rows = {}
        for rn in ("mu", "t1", "a", "b2"):
            rows["q" + rn] = sbl(f"d_r_{rn}", [1, TDL], F32)

        out_ps = [ps(f"d_o{k}", [128, TDL]) for k in range(2)]
        S_ps = ps("d_S", [1, TDL])
        SS_ps = ps("d_SS", [1, TDL])
        bca_ps = ps("d_bca", [128, TDL])
        bcb_ps = ps("d_bcb", [128, TDL])

        d_ldv = sem("d_ldv")
        d_ldz = [sem("d_ldz0"), sem("d_ldz1")]
        d_ldx = [sem("d_ldx0"), sem("d_ldx1")]
        d_ldw = [sem("d_ldw0"), sem("d_ldw1")]
        d_mm = sem("d_mm")
        d_ev = sem("d_ev")
        d_add = sem("d_add")
        d_sq = sem("d_sq")
        d_stat = sem("d_stat")
        d_rv = sem("d_rv")
        d_ra = sem("d_ra")
        d_rows = sem("d_rows")
        d_bc = sem("d_bc")
        d_norm = sem("d_norm")
        st_out = [env["psem"]("d_sto0"), env["psem"]("d_sto1")]
        ch_vd = sem("d_chv")
        ch_sd = sem("d_chs")

        rows["_S"] = S_ps
        rows["_SS"] = SS_ps

        @block.sync
        def _(s):
            _bs(s, bar_in, env)
            s.dma_start(out=v_sd, in_=env["vsd"]).then_inc(d_ldv, 16)
            for i in range(NTDL):
                pa = i % 2
                W(s, d_mm, NCD * (i - 1))
                s.dma_start(
                    out=z_sb[pa].rearrange("p (c t) -> p c t", c=NCH),
                    in_=env["sc_z"].rearrange("(c p) t -> p c t", p=128)[
                        :, :, i * TDL:(i + 1) * TDL]).then_inc(d_ldz[pa], 16)
                W(s, st_out[pa], 16 * (i // 2))
                s.dma_start(
                    out=xu_t[pa].rearrange("p (c t) -> p c t", c=NCD),
                    in_=env["sc_x1f"].rearrange("(c p) t -> p c t", p=128)[
                        :, :, i * TDL:(i + 1) * TDL]).then_inc(d_ldx[pa], 16)
                for c in range(NCD):
                    gg = NCD * i + c
                    W(s, d_mm, gg - 1)
                    s.dma_start(
                        out=wd_c[gg % 2],
                        in_=env["sc_wd"][c * 128:(c + 1) * 128, :],
                    ).then_inc(d_ldw[gg % 2], 16)
            s.sem_inc(bar_out, 1)

        @block.tensor
        def _(t):
            _bs(t, bar_in, env)
            t.wait_ge(env["c_ini"], 1)
            pend = [None]

            def flush_dbc(t):
                if pend[0] is None:
                    return
                ii = pend[0]
                pend[0] = None
                t.wait_ge(d_rows, ii + 1)
                W(t, d_norm, NCD * ii)
                t.matmul(bca_ps, onesr_f, rows["qa"], start=True, stop=True)
                t.matmul(bcb_ps, onesr_f, rows["qb2"],
                         start=True, stop=True).then_inc(d_bc, 1)

            for i in range(NTDL):
                pa = i % 2
                t.wait_ge(d_ldz[pa], 16 * (i // 2 + 1))
                for c in range(NCD):
                    gg = NCD * i + c
                    t.wait_ge(d_ldw[gg % 2], 16 * (gg // 2 + 1))
                    W(t, d_ev, gg - 1)
                    for hc in range(NCH):
                        mi = t.matmul(out_ps[gg % 2],
                                      wd_c[gg % 2][:, hc * 128:(hc + 1) * 128],
                                      z_sb[pa][:, hc * TDL:(hc + 1) * TDL],
                                      start=(hc == 0), stop=(hc == NCH - 1))
                    mi.then_inc(d_mm, 1)
                    if c >= 1:
                        if c == 1:
                            flush_dbc(t)
                        _d_stats(t, env, i, c - 1, d_sq, d_stat, d_rows,
                                 ub_scr, sq_scr, S_ps, SS_ps, ones_f)
                _d_stats(t, env, i, NCD - 1, d_sq, d_stat, d_rows,
                         ub_scr, sq_scr, S_ps, SS_ps, ones_f)
                pend[0] = i
            flush_dbc(t)
            t.wait_ge(d_bc, NTDL)
            t.sem_inc(bar_out, 1)

        @block.scalar
        def _(sc):
            _bs(sc, bar_in, env)
            cs = Chain(sc, ch_sd)
            sc.wait_ge(d_ldv, 16)
            for i in range(NTDL):
                for c in range(NCD):
                    gg = NCD * i + c
                    sc.wait_ge(d_mm, gg + 1)
                    W(sc, d_add, gg - 1)
                    sc.activation(d_sb[gg % 2], out_ps[gg % 2], AF.Identity,
                                  scale=v_sd[:, c:c + 1]).then_inc(d_ev, 1)
                _c_rows_act(sc, cs, rows, d_rv, d_ra, i + 1)
            sc.wait_ge(d_ra, NTDL)
            sc.sem_inc(bar_out, 1)

        @block.vector
        def _(v):
            _bs(v, bar_in, env)
            cv = Chain(v, ch_vd)
            for i in range(NTDL):
                pa = i % 2
                v.wait_ge(d_ldx[pa], 16 * (i // 2 + 1))
                for c in range(NCD):
                    gg = NCD * i + c
                    v.wait_ge(d_ev, gg + 1)
                    xc = xu_t[pa][:, c * TDL:(c + 1) * TDL]
                    cv(v.tensor_add(xc, xc, d_sb[gg % 2]))
                    v.engine_nop().then_inc(d_add, 1)
                    W(v, d_stat, gg - 1)
                    v.tensor_copy(ub_scr[gg % 2], xc)
                    v.tensor_mul(sq_scr[gg % 2], xc, xc).then_inc(d_sq, 1)
                    if c == NCD - 1:
                        _c_rows_dve(v, cv, rows, 2048, d_stat, NCD * (i + 1),
                                    d_ra, i + 1, d_rv, d_rows, d_bc, i)
                for c in range(NCD):
                    v.wait_ge(d_bc, i + 1)
                    W(v, st_out[pa], 16 * (i // 2))
                    xc = xu_t[pa][:, c * TDL:(c + 1) * TDL]
                    cv(v.tensor_mul(xc, xc, bca_ps))
                    v.tensor_add(xc, xc, bcb_ps).then_inc(d_norm, 1)
            v.wait_ge(d_norm, NCD * NTDL)
            v.sem_inc(bar_out, 1)

        @block.gpsimd
        def _(g):
            _bs(g, bar_in, env)
            for i in range(NTDL):
                pa = i % 2
                g.wait_ge(d_norm, NCD * (i + 1))
                g.dma_start(
                    out=env["outT"].rearrange("(c p) t -> p c t", p=128)[
                        :, :, i * TDL:(i + 1) * TDL],
                    in_=xu_t[pa].rearrange("p (c t) -> p c t", c=NCD),
                ).then_inc(st_out[pa], 16)
            for sm in st_out:
                g.wait_ge(sm, 16 * (NTDL // 2))
            g.sem_inc(bar_out, 1)


def _d_stats(t, env, i, c, d_sq, d_stat, d_rows, ub, sq_scr, S_ps, SS_ps, ones_f):
    gg = NCD * i + c
    t.wait_ge(d_sq, gg + 1)
    if c == 0:
        W(t, d_rows, i)
    t.matmul(S_ps, env["ones_bf"], ub[gg % 2],
             start=(c == 0), stop=(c == NCD - 1))
    t.matmul(SS_ps, env["ones_bf"], sq_scr[gg % 2],
             start=(c == 0), stop=(c == NCD - 1)).then_inc(d_stat, 1)


def _chunkvec(v, n):
    out = np.zeros((n, 128), np.float32)
    vv = np.asarray(v, np.float32)
    out.flat[: vv.size] = vv
    return np.ascontiguousarray(out.T)


def make_inputs(inp, b):
    x = np.asarray(inp["L0_attn_x"])
    d = {}
    d["xT"] = np.ascontiguousarray(np.asarray(x[b], np.float32).T)
    d["wdqT"] = np.ascontiguousarray(np.asarray(inp["L0_attn_W_d_q"], np.float32).T)
    d["wuqT"] = np.ascontiguousarray(np.asarray(inp["L0_attn_W_u_q"], np.float32).T)
    d["wdkvT"] = np.ascontiguousarray(np.asarray(inp["L0_attn_W_d_kv"], np.float32).T)
    perm = [h * 192 + j for h in range(H) for j in range(64)] + \
           [h * 192 + 64 + j for h in range(H) for j in range(128)]
    d["wukvT"] = np.ascontiguousarray(np.asarray(inp["L0_attn_W_u_kv"], np.float32).T[:, perm])
    d["woT"] = np.ascontiguousarray(np.asarray(inp["L0_attn_W_o"], np.float32).T)
    d["wgT"] = np.ascontiguousarray(np.asarray(inp["L0_ffn_wq_gate"], np.int32).T)
    d["wuT2"] = np.ascontiguousarray(np.asarray(inp["L0_ffn_wq_up"], np.int32).T)
    d["wdT"] = np.ascontiguousarray(np.asarray(inp["L0_ffn_wq_down"], np.int32).T)
    d["vdq"] = _chunkvec(inp["L0_attn_b_d_q"], NCQ)
    d["vqg"] = _chunkvec(inp["L0_attn_q_norm_g"], NCQ)
    d["vqb"] = _chunkvec(inp["L0_attn_q_norm_b"], NCQ)
    d["vuq"] = _chunkvec(inp["L0_attn_b_u_q"], NCD)
    d["vdkv"] = _chunkvec(inp["L0_attn_b_d_kv"], NCKV)
    d["vkvg"] = _chunkvec(inp["L0_attn_kv_norm_g"], NCKN)
    d["vkvb"] = _chunkvec(inp["L0_attn_kv_norm_b"], NCKN)
    d["vukv"] = _chunkvec(np.asarray(inp["L0_attn_b_u_kv"], np.float32)[perm], NCKVU)
    d["vo"] = _chunkvec(inp["L0_attn_b_o"], NCD)
    d["vsg"] = _chunkvec(inp["L0_ffn_s_gate"], NCH)
    d["vsu"] = _chunkvec(inp["L0_ffn_s_up"], NCH)
    d["vsd"] = _chunkvec(inp["L0_ffn_s_down"], NCD)
    d["c_ident"] = np.eye(128, dtype=np.float32)
    m = np.zeros((128, 4 * TB), np.float32)
    p = np.arange(128)[:, None]
    f = np.arange(TB)[None, :]
    for k in range(4):
        m[:, k * TB:(k + 1) * TB] = ((p + k * 128) <= f).astype(np.float32)
    d["c_masks"] = m
    jj = np.arange(32)
    invf = 10000.0 ** (-jj / 32.0)
    ang = np.arange(H)[:, None] * invf[None, :]
    cosW = np.cos(ang).astype(np.float32)
    sinW = np.sin(ang).astype(np.float32)
    d["c_foldc"] = np.ascontiguousarray(
        np.broadcast_to(cosW.reshape(1, 512), (128, 512)).astype(np.float32))
    d["c_folds"] = np.ascontiguousarray(
        np.broadcast_to(sinW.reshape(1, 512), (128, 512)).astype(np.float32))
    d["c_bfc"] = np.ascontiguousarray(cosW)
    d["c_bfs"] = np.ascontiguousarray(sinW)
    sel = np.zeros((128, 64), np.float32)
    for j in range(8):
        sel[:, 8 * j + j] = 1.0
    d["c_sel"] = sel
    return d


def kernel(**inp):
    nc = build_program()
    in_maps = [make_inputs(inp, b) for b in range(8)]
    res = run_bass_kernel_spmd(nc, in_maps, core_ids=list(range(8)))
    out = np.stack([np.asarray(r["outT"]).T for r in res.results], axis=0)
    return np.ascontiguousarray(out.astype(np.float32, copy=False))
